# version 02
# baseline (speedup 1.0000x reference)
import numpy as np
import ml_dtypes
import concourse.bass as bass
import concourse.mybir as mybir
from concourse.bass_utils import run_bass_kernel_spmd

F32 = mybir.dt.float32
BF16 = mybir.dt.bfloat16
AF = mybir.ActivationFunctionType
ALU = mybir.AluOpType
AX = mybir.AxisListType
BF = ml_dtypes.bfloat16

T = 2048
D = 1024
NT = 16
NDC = 8
NH = 16
HD = 64
FD = 256
DFF0 = 4096
NE = 8
DFF1 = 2048

ENGINES = ("sync", "scalar", "vector", "tensor", "gpsimd")

UNFUSED = False
DETECT_RACES = True
DUMP_SBUF = False
DBG_BAR = 0
SKIP = set()
L0A_STEPS = 99
L0A_MASK = None


def pieces(lo, hi, step=512):
    out = []
    a = lo
    while a < hi:
        b = min(hi, (a // step + 1) * step)
        out.append((a, b))
        a = b
    return out


class Sched:
    def __init__(self, nc):
        self.nc = nc
        self.phases = [[]]
        self.semh = {}
        self.semc = {}

    def sem(self, name):
        if name not in self.semh:
            self.semh[name] = self.nc.semaphore(name).__enter__()
            self.semc[name] = 0
        return name

    def c(self, name):
        self.sem(name)
        return self.semc[name]

    def phase(self, label=""):
        if self.phases[-1]:
            self.phases.append([])
        self.labels = getattr(self, "labels", [])
        self.labels.append((len(self.phases) - 1, label))

    def op(self, eng, fn, waits=(), inc=None, inc_amt=1):
        for (s, v) in waits:
            self.sem(s)
            assert v <= self.semc[s], f"wait on future {s}={v}>{self.semc[s]}"
        post = None
        if inc is not None:
            self.sem(inc)
            self.semc[inc] += inc_amt
            post = (inc, inc_amt)
        self.phases[-1].append(
            (eng, tuple((s, v) for s, v in waits if v > 0), fn, post))
        return self.semc[inc] if inc else None

    def dma(self, eng, out_ap, in_ap, sem, waits=()):
        self.rr = getattr(self, "rr", {})
        k = (sem, eng)
        i = self.rr.get(k, 0)
        self.rr[k] = i + 1
        s = f"{sem}@{eng}#{i % 8}"
        def fn(e, o=out_ap, i=in_ap):
            return e.dma_start(o, i)
        v = self.op(eng, fn, waits=waits, inc=self.sem(s), inc_amt=16)
        return (s, v)

    def dtok(self, sem):
        return [(s, v) for s, v in self.semc.items()
                if s.startswith(sem + "@") and v > 0]

    def emit(self):
        nc = self.nc
        for recs in self.phases:
            if not recs:
                continue
            with nc.Block() as block:
                for ename in ENGINES:
                    rs = [r for r in recs if r[0] == ename]
                    if rs:
                        self._emit_engine(block, ename, rs)

    def _emit_engine(self, block, ename, rs):
        semh = self.semh
        waited = {}

        def body(eng):
            for (_e, waits, fn, post) in rs:
                for (s, v) in waits:
                    if waited.get(s, -1) >= v:
                        continue
                    eng.wait_ge(semh[s], v)
                    waited[s] = v
                ins = fn(eng)
                if post is not None:
                    ins.then_inc(semh[post[0]], post[1])
        getattr(block, ename)(body)


def build_core_program(layers=(0, 1), taps=(), stop_after=None,
                       rms_ntiles=NT):
    nc = bass.Bass("TRN2", target_bir_lowering=False,
                   detect_race_conditions=DETECT_RACES)
    S = Sched(nc)
    taps = set(taps)
    dram = {}

    class Locals:

        def __init__(self):
            self.guards = []

        def sb(self, name, shape, dtype):
            g = nc.sbuf_tensor(name, list(shape), dtype)
            h = g.__enter__()
            self.guards.append(g)
            return h

        def ps(self, name, shape, dtype=F32):
            g = nc.psum_tensor(name, list(shape), dtype)
            h = g.__enter__()
            self.guards.append(g)
            return h

        def done(self):
            for g in reversed(self.guards):
                g.__exit__(None, None, None)
            self.guards = []

    def din(name, shape, dtype=F32):
        dram[name] = nc.dram_tensor(name, list(shape), dtype,
                                    kind="ExternalInput").ap()
        return dram[name]

    def dout(name, shape, dtype=F32):
        dram[name] = nc.dram_tensor(name, list(shape), dtype,
                                    kind="ExternalOutput").ap()
        return dram[name]

    def dint(name, shape, dtype):
        kind = "ExternalOutput" if name in taps else "Internal"
        dram[name] = nc.dram_tensor(name, list(shape), dtype, kind=kind).ap()
        return dram[name]

    if 0 in layers:
        din("x", (T, D))
        for n in ("L0WqT", "L0WkT", "L0WvT", "L0WoT"):
            din(n, (D, D))
        din("L0WfT", (HD, FD))
        din("L0gw", (D, NE))
        din("L0w1", (NE, D, DFF0))
        din("L0b1", (NE, 32, 128))
        din("L0w2", (NE, DFF0, D))
        din("L0b2", (NE, D))
    if 1 in layers:
        if 0 not in layers:
            din("xin1", (T, D))
        for n in ("L1WqT", "L1WkT", "L1WvT", "L1WoT",
                  "F1WqT", "F1WkT", "F1WvT", "F1WoT"):
            din(n, (D, D))
        din("gammas", (1, NH))
        din("ln1w", (1, D))
        din("ln2w", (1, D))
        din("F1gwT", (D, NE))
        din("F1w1T", (NE, D, DFF1))
        din("F1w3T", (NE, D, DFF1))
        din("F1w2T", (NE, DFF1, D))
    din("IDBF", (128, 128), BF16)
    din("IDF32", (128, 128), F32)
    din("ONESROW", (1, 128), F32)
    din("ONESROWB", (1, 128), BF16)
    din("ONESCOL", (128, 1), BF16)
    din("PIDX", (128, 1), F32)
    din("UROW", (128, T), BF16)
    din("CTRI", (128, 128), BF16)

    if layers == (0,):
        XRES = dout("xout", (T, D))
    elif layers == (1,):
        dout("y", (T, D))
        XRES = dram["xin1"]
    else:
        dout("y", (T, D))
        XRES = dint("xres", (T, D), F32)
    if 0 in layers:
        S0 = dint("s0", (T, D), BF16)
        ATM = dint("atm", (T, D), BF16)
    if 1 in layers:
        S1 = dint("s1", (T, D), BF16)
        XN1 = dint("xn1", (T, D), BF16)
        H1 = dint("h1", (T, D), BF16)
        ASP = dint("asp", (NT, D, 128), BF16)
        GD = dint("gd", (T, D), F32)

    def sbuf(name, shape, dtype):
        h = nc.sbuf_tensor(name, list(shape), dtype).__enter__()
        return h

    B0 = sbuf("B0", (128, 16640), BF16)
    B1 = sbuf("B1", (128, 16384), BF16)
    B2 = sbuf("B2", (128, 16384), BF16)
    B3 = sbuf("B3", (128, 16384), BF16)
    WS = sbuf("WS", (128, 8192), BF16)
    SCA = sbuf("SCA", (128, 1024), F32)
    SCB = sbuf("SCB", (128, 1024), F32)
    SCC = sbuf("SCC", (128, 2048), F32)
    ST = sbuf("ST", (128, 512), F32)
    M8 = sbuf("M8", (128, 128), F32)
    IDBF = sbuf("c_idbf", (128, 128), BF16)
    IDF32 = sbuf("c_idf32", (128, 128), F32)
    ONESROW = sbuf("c_onesrow", (1, 128), F32)
    ONESROWB = sbuf("c_onesrowb", (1, 128), BF16)
    ONESCOL = sbuf("c_onescol", (128, 1), BF16)
    PIDX = sbuf("c_pidx", (128, 1), F32)
    FB = sbuf("c_fbias", (128, 4), F32)
    SM = sbuf("SM", (128, 64), F32)
    LNW = sbuf("LNW", (128, 1024), F32)
    XTA = sbuf("XTA", (128, 1024), BF16)
    XTB = sbuf("XTB", (128, 1024), BF16)
    WOB = sbuf("WOB", (128, 1024), BF16)
    ASL = sbuf("ASL", (128, 1024), BF16)
    ADD = sbuf("ADD", (128, 1024), BF16)

    def load_consts():
        S.phase("consts")
        for t_, n_ in ((IDBF, "IDBF"), (IDF32, "IDF32"), (ONESROW, "ONESROW"),
                       (ONESROWB, "ONESROWB"), (ONESCOL, "ONESCOL"),
                       (PIDX, "PIDX")):
            S.dma("sync", t_[:, :], dram[n_], "d_ld")
        S.op("vector", lambda e: e.memset(FB[:, 0:1], 1e-5), inc="vec")
        S.op("vector", lambda e: e.memset(FB[:, 1:2], 1e-30), inc="vec")
        S.op("sync", lambda e: e.nop(), waits=S.dtok("d_ld"))

    def norm_phase(src_dram, fm_buf, tm_dram, tag, kind, w_dram=None, ntiles=NT):
        S.phase(tag)
        L = Locals()
        PTh = L.ps("pt_" + tag, [128, 1024], BF16)
        if kind == "rms":
            PWh = L.ps("pw_" + tag, [128, 1024], F32)
            ROWC = L.sb("rw_" + tag, [1, 1024], F32)
            ldw = S.dma("sync", ROWC[:, :], w_dram, "d_ld")
            last = None
            for seg in range(2):
                last = S.op("tensor", lambda e, seg=seg: e.matmul(
                    PWh[:, seg * 512:(seg + 1) * 512], ONESROW[:, :],
                    ROWC[:, seg * 512:(seg + 1) * 512],
                    start=True, stop=True), waits=[ldw],
                    inc="mm")
                S.op("vector", lambda e, seg=seg: e.tensor_copy(
                    LNW[:, seg * 512:(seg + 1) * 512],
                    PWh[:, seg * 512:(seg + 1) * 512]),
                    waits=[("mm", last)], inc="vec")
        st_after = {}
        for t in range(max(0, ntiles)):
            if kind == "ln":
                buf = (SCA, SCB)[t % 2]
            else:
                buf = (XTA, XTB)[t % 2]
            xnb = (XTA, XTB)[t % 2] if kind == "ln" else None
            ldw_ = ([("vec", S.c("vec")), ("mm", S.c("mm"))] if t >= 2 else [])
            if tm_dram is not None and t >= 2:
                ldw_.append(st_after[t - 2])
            ldv = S.dma("sync", buf[:, :], src_dram[t * 128:(t + 1) * 128, :],
                        "d_ld", waits=ldw_)
            if kind == "ln":
                S.op("vector", lambda e, b=buf, t=t: e.reduce_sum(
                    ST[:, t:t + 1], b[:, :], axis=AX.X), waits=[ldv])
                S.op("vector", lambda e, b=buf: e.tensor_tensor(
                    SCC[:, 0:1024], b[:, :], b[:, :], ALU.mult))
                S.op("vector", lambda e, t=t: e.reduce_sum(
                    ST[:, 16 + t:17 + t], SCC[:, 0:1024], axis=AX.X))
                pv = S.op("vector", lambda e, t=t: e.tensor_scalar(
                    ST[:, 32 + t:33 + t], ST[:, t:t + 1], 1.0 / D, None,
                    ALU.mult), waits=[("vec", S.c("vec"))], inc="vec")
                pv = S.op("vector", lambda e, t=t: e.tensor_tensor(
                    ST[:, 48 + t:49 + t], ST[:, 32 + t:33 + t],
                    ST[:, 32 + t:33 + t], ALU.mult),
                    waits=[("vec", pv)], inc="vec")
                pv = S.op("vector", lambda e, t=t: e.tensor_scalar(
                    ST[:, 64 + t:65 + t], ST[:, 16 + t:17 + t], 1.0 / D,
                    None, ALU.mult), waits=[("vec", pv)], inc="vec")
                vv = S.op("vector", lambda e, t=t: e.tensor_tensor(
                    ST[:, 80 + t:81 + t], ST[:, 64 + t:65 + t],
                    ST[:, 48 + t:49 + t], ALU.subtract),
                    waits=[("vec", pv)], inc="vec")
                S.op("scalar", lambda e, t=t: e.activation(
                    ST[:, 96 + t:97 + t], ST[:, 80 + t:81 + t], AF.Ln, bias=FB[:, 0:1]),
                    waits=[("vec", vv)])
                aa = S.op("scalar", lambda e, t=t: e.activation(
                    ST[:, 112 + t:113 + t], ST[:, 96 + t:97 + t], AF.Exp,
                    scale=-0.5), inc="act")
                vv2 = S.op("vector", lambda e, b=buf, xb=xnb, t=t: e.tensor_scalar(
                    xb[:, :], b[:, :], ST[:, 32 + t:33 + t],
                    ST[:, 112 + t:113 + t], ALU.subtract, ALU.mult),
                    waits=[("act", aa)], inc="vec")
                tsrc = xnb
            else:
                S.op("vector", lambda e, b=buf: e.tensor_tensor(
                    SCC[:, 0:1024], b[:, :], b[:, :], ALU.mult),
                    waits=[ldv])
                vv = S.op("vector", lambda e, t=t: e.reduce_sum(
                    ST[:, t:t + 1], SCC[:, 0:1024], axis=AX.X), inc="vec")
                S.op("scalar", lambda e, t=t: e.activation(
                    ST[:, 16 + t:17 + t], ST[:, t:t + 1], AF.Ln, bias=FB[:, 0:1],
                    scale=1.0 / D), waits=[("vec", vv)])
                aa = S.op("scalar", lambda e, t=t: e.activation(
                    ST[:, 32 + t:33 + t], ST[:, 16 + t:17 + t], AF.Exp,
                    scale=-0.5), inc="act")
                S.op("vector", lambda e, b=buf, t=t: e.tensor_scalar(
                    SCC[:, 0:1024], b[:, :], ST[:, 32 + t:33 + t], None,
                    ALU.mult), waits=[("act", aa)])
                vv2 = S.op("vector", lambda e, b=buf: e.tensor_tensor(
                    b[:, :], SCC[:, 0:1024], LNW[:, 0:1024], ALU.mult),
                    inc="vec")
                tsrc = buf
            if tm_dram is not None:
                st_after[t] = S.dma("scalar",
                                    tm_dram[t * 128:(t + 1) * 128, :],
                                    tsrc[:, :], "d_st", waits=[("vec", vv2)])
            last = None
            for c in range(NDC):
                last = S.op("tensor", lambda e, xb=tsrc, c=c: e.transpose(
                    PTh[:, c * 128:(c + 1) * 128], xb[:, c * 128:(c + 1) * 128],
                    IDBF[:, :]), waits=([("vec", vv2)] if c == 0 else []),
                    inc=("mm" if c == NDC - 1 else None))

            def evac(e, t=t):
                o = fm_buf[:, 0:16384].rearrange("p (c n) -> p c n", n=2048)
                o = o[:, :, t * 128:(t + 1) * 128]
                i = PTh[:, :].rearrange("p (c j) -> p c j", j=128)
                return e.tensor_copy(o, i)
            S.op("vector", evac, waits=[("mm", last)], inc="vec")
        if tm_dram is not None:
            S.op("scalar", lambda e: e.nop(), waits=S.dtok("d_st"))
        L.done()

    def vtm_phase(fm_buf, wT_dram, out_buf, aug65, tag, scale=None):
        S.phase(tag)
        with nc.psum_tensor("pva_" + tag, [128, 512], F32) as PA, \
             nc.psum_tensor("pvb_" + tag, [128, 512], F32) as PB:
            src = wT_dram.rearrange("(c p) n -> p c n", p=128)
            dst = WS[:, 0:8192].rearrange("p (c n) -> p c n", n=1024)
            ldv = S.dma("gpsimd", dst, src, "d_ld")
            if aug65:
                o = out_buf[:, 0:NT * 1040].rearrange("p (t h n) -> p t h n",
                                                      h=NH, n=65)
                S.op("vector", lambda e, o=o: e.memset(o[:, :, :, 64:65], 1.0),
                     inc="vec")
            evs = {}
            for t in range(NT):
                for seg in range(2):
                    P = (PA, PB)[(t * 2 + seg) % 2]
                    last = None
                    for i in range(NDC):
                        w_ = []
                        if t == 0 and seg == 0 and i == 0:
                            w_.append(ldv)
                        if i == 0 and (t * 2 + seg) >= 2:
                            w_.append(("vec", evs[(t * 2 + seg) - 2]))
                        last = S.op("tensor", lambda e, i=i, t=t, seg=seg, P=P:
                                    e.matmul(
                            P[:, :],
                            fm_buf[:, i * 2048 + t * 128:i * 2048 + (t + 1) * 128],
                            WS[:, i * 1024 + seg * 512:i * 1024 + (seg + 1) * 512],
                            start=(i == 0), stop=(i == NDC - 1)),
                            waits=w_, inc=("mm" if i == NDC - 1 else None))
                    if aug65:
                        def ev(e, t=t, seg=seg, P=P):
                            o = out_buf[:, t * 1040:(t + 1) * 1040]
                            o = o.rearrange("p (h n) -> p h n", n=65)
                            o = o[:, seg * 8:(seg + 1) * 8, 0:64]
                            i_ = P[:, :].rearrange("p (h n) -> p h n", n=64)
                            return e.tensor_copy(o, i_)
                        evs[t * 2 + seg] = S.op("vector", ev,
                                                waits=[("mm", last)], inc="vec")
                    elif scale is None:
                        evs[t * 2 + seg] = S.op(
                            "vector", lambda e, t=t, seg=seg, P=P: e.tensor_copy(
                                out_buf[:, t * 1024 + seg * 512:
                                        t * 1024 + (seg + 1) * 512], P[:, :]),
                            waits=[("mm", last)], inc="vec")
                    else:
                        evs[t * 2 + seg] = S.op(
                            "vector", lambda e, t=t, seg=seg, P=P: e.tensor_scalar(
                                out_buf[:, t * 1024 + seg * 512:
                                        t * 1024 + (seg + 1) * 512],
                                P[:, :], scale, None, ALU.mult),
                            waits=[("mm", last)], inc="vec")

    def qk_chunk(fm_buf, wT_dram, c, out_sb, PA, PB, scale=None, wslot=0):
        src = wT_dram[:, c * 128:(c + 1) * 128].rearrange("(i p) n -> p i n",
                                                          p=128)
        dst = WS[:, wslot:wslot + 1024].rearrange("p (i n) -> p i n", n=128)
        ldv = S.dma("gpsimd", dst, src, "d_ld",
                    waits=[("mm", S.c("mm"))])
        evs = {}
        for seg in range(4):
            P = (PA, PB)[seg % 2]
            last = None
            for i in range(NDC):
                w_ = []
                if seg == 0 and i == 0:
                    w_.append(ldv)
                if i == 0 and seg >= 2:
                    w_.append(("vec", evs[seg - 2]))
                last = S.op("tensor", lambda e, i=i, seg=seg, P=P: e.matmul(
                    P[:, :], WS[:, wslot + i * 128:wslot + (i + 1) * 128],
                    fm_buf[:, i * 2048 + seg * 512:i * 2048 + (seg + 1) * 512],
                    start=(i == 0), stop=(i == NDC - 1)),
                    waits=w_, inc=("mm" if i == NDC - 1 else None))
            if scale is None:
                evs[seg] = S.op("vector", lambda e, seg=seg, P=P: e.tensor_copy(
                    out_sb[:, seg * 512:(seg + 1) * 512], P[:, :]),
                    waits=[("mm", last)], inc="vec")
            else:
                evs[seg] = S.op("vector", lambda e, seg=seg, P=P: e.tensor_scalar(
                    out_sb[:, seg * 512:(seg + 1) * 512], P[:, :], scale, None,
                    ALU.mult), waits=[("mm", last)], inc="vec")

    def wo_proj(src_mode, wT_dram, add_dram, out_dram, tag):
        S.phase(tag)
        with nc.psum_tensor("pwt_" + tag, [128, 1024], BF16) as PT, \
             nc.psum_tensor("pwo_" + tag, [128, 1024], F32) as PO:
            src = wT_dram.rearrange("(c p) n -> p c n", p=128)
            dst = WS[:, 0:8192].rearrange("p (c n) -> p c n", n=1024)
            ldv = S.dma("gpsimd", dst, src, "d_ld")
            st_prev = ("d_st@sync", 0)
            for t in range(NT):
                if add_dram is not None:
                    lda = S.dma("sync", ADD[:, :],
                                add_dram[t * 128:(t + 1) * 128, :], "d_ld",
                                waits=([("vec", S.c("vec"))] if t >= 1 else ()))
                if src_mode[0] == "dram_tm":
                    ldt = S.dma("scalar", XTA[:, :],
                                src_mode[1][t * 128:(t + 1) * 128, :], "d_ld",
                                waits=([("mm", S.c("mm"))] if t >= 1 else ()))
                    last = None
                    for c in range(NDC):
                        last = S.op("tensor", lambda e, c=c: e.transpose(
                            PT[:, c * 128:(c + 1) * 128],
                            XTA[:, c * 128:(c + 1) * 128], IDBF[:, :]),
                            waits=([ldt, ("vec", S.c("vec"))]
                                   if c == 0 else []),
                            inc=("mm" if c == NDC - 1 else None))
                    avv = S.op("vector", lambda e: e.tensor_copy(
                        ASL[:, :], PT[:, :]), waits=[("mm", last)], inc="vec")
                    asl_ready = [("vec", avv)]
                else:
                    lda2 = S.dma("scalar",
                                 ASL[:, :].rearrange("p (c n) -> p c n", n=128),
                                 dram["asp"][t].rearrange("(c p) n -> p c n",
                                                          p=128),
                                 "d_ld",
                                 waits=([("mm", S.c("mm"))] if t >= 1 else ()))
                    asl_ready = [lda2]
                last = None
                for seg in range(2):
                    for c in range(NDC):
                        w_ = list(asl_ready) if (c == 0 and seg == 0) else []
                        if t == 0 and seg == 0 and c == 0:
                            w_.append(ldv)
                        if seg == 0 and c == 0 and t >= 1:
                            w_.append(("vec", S.c("vec")))
                        last = S.op("tensor", lambda e, c=c, seg=seg: e.matmul(
                            PO[:, seg * 512:(seg + 1) * 512],
                            ASL[:, c * 128:(c + 1) * 128],
                            WS[:, c * 1024 + seg * 512:c * 1024 + (seg + 1) * 512],
                            start=(c == 0), stop=(c == NDC - 1)),
                            waits=w_,
                            inc=("mm" if (seg == 1 and c == NDC - 1) else None))
                vv = None
                for seg in range(2):
                    w_ = [("mm", last), st_prev]
                    if add_dram is not None:
                        vv = S.op("vector", lambda e, seg=seg: e.tensor_tensor(
                            WOB[:, seg * 512:(seg + 1) * 512],
                            PO[:, seg * 512:(seg + 1) * 512],
                            ADD[:, seg * 512:(seg + 1) * 512], ALU.add),
                            waits=w_ + [lda], inc="vec")
                    else:
                        vv = S.op("vector", lambda e, seg=seg: e.tensor_copy(
                            WOB[:, seg * 512:(seg + 1) * 512],
                            PO[:, seg * 512:(seg + 1) * 512]),
                            waits=w_, inc="vec")
                st_prev = S.dma("sync", out_dram[t * 128:(t + 1) * 128, :],
                                WOB[:, :], "d_st", waits=[("vec", vv)])
            S.op("sync", lambda e: e.nop(), waits=S.dtok("d_st"))

    def l0_attn():
        vtm_phase(B2, dram["L0WvT"], B0, False, "l0v")
        S.phase("l0attn")
        QF = B3
        KF = B1
        QC = B3[:, 8192:10240]
        KC = B3[:, 10240:12288]
        KVS = B3[:, 12288:12416]
        KSS = B3[:, 12416:12418]
        HSTRIP = B3[:, 13312:14336]
        with nc.psum_tensor("pl0a", [128, 512], F32) as PA, \
             nc.psum_tensor("pl0b", [128, 512], F32) as PB, \
             nc.psum_tensor("pl0f", [128, 512], F32) as PF, \
             nc.psum_tensor("pl0kv", [128, 512], F32) as PKV, \
             nc.psum_tensor("pl0ks", [128, 512], F32) as PKS, \
             nc.psum_tensor("pl0o", [128, 512], F32) as PO, \
             nc.psum_tensor("pl0n", [128, 512], F32) as PN:
            ldwf1 = S.dma("gpsimd", WS[0:64, 4096:4352], dram["L0WfT"], "d_ld")
            ldwf2 = S.dma("gpsimd", WS[64:128, 4352:4608], dram["L0WfT"],
                          "d_ld")
            for c in range(NDC):
                if L0A_MASK is None or 1 in L0A_MASK:
                    qk_chunk(B2, dram["L0WqT"], c, QC, PA, PB, wslot=0)
                    qk_chunk(B2, dram["L0WkT"], c, KC, PA, PB, wslot=1024)
                for hh in range(2):
                    if L0A_STEPS < 2 and not (L0A_MASK and 9 in L0A_MASK):
                        continue
                    H_ = 2 * c + hh
                    r0 = 64 * hh
                    stt_tok = {}
                    exp_tok = {}
                    banks = (PF, PA)
                    for kk in range(24):
                        p = kk % 2
                        P_ = banks[p]
                        smin = SCC[:, 1024 * p:1024 * p + 512]
                        sexp = SCC[:, 1024 * p + 512:1024 * p + 1024]
                        w_ = []
                        if kk - 2 in stt_tok:
                            w_.append(("vec", stt_tok[kk - 2]))
                        else:
                            w_.append(("vec", S.c("vec")))
                        if kk == 0:
                            w_ += [ldwf1, ldwf2, ("act", S.c("act"))]
                        if kk < 8:
                            fc, seg = kk // 4, kk % 4
                            width = 512
                            mmv = S.op("tensor",
                                       lambda e, fc=fc, seg=seg, r0=r0, P_=P_:
                                       e.matmul(
                                           P_[:, 0:512],
                                           WS[r0:r0 + 64,
                                              4096 + r0 * 4 + fc * 128:
                                              4096 + r0 * 4 + (fc + 1) * 128],
                                           QC[r0:r0 + 64,
                                              seg * 512:(seg + 1) * 512],
                                           start=True, stop=True),
                                       waits=w_, inc="mm")
                        else:
                            t = kk - 8
                            width = 256
                            mmv = S.op("tensor",
                                       lambda e, t=t, r0=r0, P_=P_: e.matmul(
                                           P_[:, 0:256],
                                           KC[r0:r0 + 64,
                                              t * 128:(t + 1) * 128],
                                           WS[r0:r0 + 64,
                                              4096 + r0 * 4:4352 + r0 * 4],
                                           start=True, stop=True),
                                       waits=w_, inc="mm")
                        vw = [("mm", mmv)]
                        if kk - 2 in exp_tok:
                            vw.append(("act", exp_tok[kk - 2]))
                        vv = S.op("vector", lambda e, P_=P_, smin=smin,
                                  width=width: e.tensor_scalar(
                                      smin[:, 0:width], P_[:, 0:width], 0.0,
                                      None, ALU.min),
                                  waits=vw, inc="vec")
                        exp_tok[kk] = S.op("scalar", lambda e, smin=smin,
                                           sexp=sexp, width=width:
                                           e.activation(sexp[:, 0:width],
                                                        smin[:, 0:width],
                                                        AF.Exp),
                                           waits=[("vec", vv)], inc="act")
                        if kk < 8:
                            dst = QF[:, fc * 2048 + seg * 512:
                                     fc * 2048 + (seg + 1) * 512]
                        else:
                            dst = KF[:, t * 256:(t + 1) * 256]
                        stt_tok[kk] = S.op("vector", lambda e, dst=dst, P_=P_,
                                           sexp=sexp, width=width:
                                           e.scalar_tensor_tensor(
                                               dst, P_[:, 0:width], 0.0,
                                               sexp[:, 0:width],
                                               ALU.max, ALU.add),
                                           waits=[("act", exp_tok[kk])],
                                           inc="vec")
                    kfv = S.c("vec")
                    for fc in range(2 if (L0A_STEPS >= 4 if L0A_MASK is None else 4 in L0A_MASK) else 0):
                        for t in range(NT):
                            S.op("tensor", lambda e, fc=fc, t=t, H_=H_: e.matmul(
                                PKV[:, fc * 64:(fc + 1) * 64],
                                KF[:, t * 256 + fc * 128:t * 256 + (fc + 1) * 128],
                                B0[:, t * 1024 + H_ * 64:t * 1024 + (H_ + 1) * 64],
                                start=(t == 0), stop=(t == NT - 1)),
                                waits=([("vec", kfv)]
                                       if (t == 0 and fc == 0) else []),
                                inc=("mm" if (fc == 1 and t == NT - 1) else None))
                            S.op("tensor", lambda e, fc=fc, t=t: e.matmul(
                                PKS[:, fc:fc + 1],
                                KF[:, t * 256 + fc * 128:t * 256 + (fc + 1) * 128],
                                ONESCOL[:, :], start=(t == 0),
                                stop=(t == NT - 1)),
                                inc=("mm" if (fc == 1 and t == NT - 1) else None))
                    mmv = S.c("mm")
                    S.op("vector", lambda e: e.tensor_copy(
                        KVS[:, :].rearrange("p (f n) -> p f n", n=64),
                        PKV[:, 0:128].rearrange("p (f n) -> p f n", n=64)),
                        waits=[("mm", mmv)])
                    vv = S.op("vector", lambda e: e.tensor_copy(
                        KSS[:, :], PKS[:, 0:2]), inc="vec")
                    if DBG_BAR >= 1:
                        S.phase("l0attn")
                    ev_prev = 0
                    for t in range(NT):
                        b8 = t % 8
                        for fc in range(2):
                            w_ = []
                            if fc == 0 and t % 8 == 0:
                                w_ = [("vec", ev_prev if t else 0)]
                                if t == 0:
                                    w_ = [("vec", vv)]
                            S.op("tensor", lambda e, t=t, fc=fc, b8=b8: e.matmul(
                                PO[:, b8 * 64:(b8 + 1) * 64],
                                QF[:, fc * 2048 + t * 128:fc * 2048 + (t + 1) * 128],
                                KVS[:, fc * 64:(fc + 1) * 64],
                                start=(fc == 0), stop=(fc == 1)),
                                waits=w_, inc=("mm" if fc == 1 else None))
                            S.op("tensor", lambda e, t=t, fc=fc, b8=b8: e.matmul(
                                PN[:, b8:b8 + 1],
                                QF[:, fc * 2048 + t * 128:fc * 2048 + (t + 1) * 128],
                                KSS[:, fc:fc + 1], start=(fc == 0),
                                stop=(fc == 1)),
                                inc=("mm" if fc == 1 else None))
                        if b8 == 7:
                            mmv = S.c("mm")
                            sv = S.op("vector", lambda e: e.tensor_scalar(
                                SM[:, 0:8], PN[:, 0:8], 1e-6, None, ALU.add),
                                waits=[("mm", mmv)], inc="vec")
                            rv = S.op("vector", lambda e: e.reciprocal(
                                SM[:, 8:16], SM[:, 0:8]),
                                waits=[("vec", sv)], inc="vec")
                            for j8 in range(8):
                                tt = t - 7 + j8
                                ev_prev = S.op(
                                    "vector", lambda e, tt=tt, j8=j8:
                                    e.tensor_scalar(
                                        HSTRIP[:, tt * 64:(tt + 1) * 64],
                                        PO[:, j8 * 64:(j8 + 1) * 64],
                                        SM[:, 8 + j8:9 + j8], None, ALU.mult),
                                    waits=[("vec", rv)], inc="vec")
                    if (L0A_STEPS < 6 if L0A_MASK is None else 6 not in L0A_MASK):
                        continue
                    dstrip = dram["atm"][:, H_ * 64:(H_ + 1) * 64]
                    dstrip = dstrip.rearrange("(t p) n -> p t n", p=128)
                    sstrip = HSTRIP[:, :].rearrange("p (t n) -> p t n", n=64)
                    S.dma("sync", dstrip, sstrip, "d_st",
                          waits=[("vec", S.c("vec"))])
                    S.op("vector", lambda e: e.engine_nop(),
                         waits=S.dtok("d_st"), inc="vec")
        wo_proj(("dram_tm", dram["atm"]), dram["L0WoT"], None, S0, "l0wo")

    def l0_router():
        S.phase("l0rt")
        L = Locals()
        PL = L.ps("plog", [128, 512], F32)
        PM = L.ps("pm8f", [8, 128], F32)
        GW = L.sb("GWsb", [128, 64], BF16)
        LGT = L.sb("LGT", [128, 8], F32)
        EQ = L.sb("EQsb", [128, 8], F32)
        NS = L.sb("NSsb", [128, 4], F32)
        M8F = WS[0:8, 0:2048]
        src = dram["L0gw"].rearrange("(c p) n -> p c n", p=128)
        ldv = S.dma("gpsimd", GW[:, :].rearrange("p (c n) -> p c n", n=8),
                    src, "d_ld")
        for t in range(NT):
            last = None
            for i in range(NDC):
                last = S.op("tensor", lambda e, t=t, i=i: e.matmul(
                    PL[:, 0:8], B2[:, i * 2048 + t * 128:i * 2048 + (t + 1) * 128],
                    GW[:, i * 8:(i + 1) * 8], start=(i == 0),
                    stop=(i == NDC - 1)),
                    waits=([ldv] if (t == 0 and i == 0) else []) +
                          ([("vec", S.c("vec"))] if i == 0 else []),
                    inc=("mm" if i == NDC - 1 else None))
            lv = S.op("vector", lambda e: e.tensor_copy(LGT[:, :], PL[:, 0:8]),
                      waits=[("mm", last)], inc="vec")
            vv = S.op("vector", lambda e: e.reduce_max(
                NS[:, 0:1], LGT[:, :], axis=AX.X, negate=True),
                waits=[("vec", lv)], inc="vec")
            aa = S.op("scalar", lambda e: e.activation(
                SCC[:, 0:8], LGT[:, :], AF.Exp, bias=NS[:, 0:1],
                accum_out=NS[:, 1:2]), waits=[("vec", vv)], inc="act")
            rv = S.op("vector", lambda e: e.reciprocal(NS[:, 2:3], NS[:, 1:2]),
                      waits=[("act", aa)], inc="vec")
            pv = S.op("vector", lambda e: e.tensor_scalar(
                EQ[:, :], LGT[:, :], NS[:, 0:1], 0.0, ALU.add, ALU.is_ge),
                waits=[("vec", rv)], inc="vec")
            pv = S.op("vector", lambda e: e.memset(NS[:, 3:4], 1.0),
                      waits=[("vec", pv)], inc="vec")
            for ei in range(NE):
                pv = S.op("vector", lambda e, ei=ei: e.tensor_tensor(
                    SCC[:, 8:9], EQ[:, ei:ei + 1], NS[:, 3:4], ALU.mult),
                    waits=[("vec", pv)], inc="vec")
                pv = S.op("vector", lambda e, ei=ei, t=t: e.tensor_tensor(
                    M8[:, t * 8 + ei:t * 8 + ei + 1], SCC[:, 8:9], NS[:, 2:3],
                    ALU.mult), waits=[("vec", pv)], inc="vec")
                pv = S.op("vector", lambda e: e.tensor_tensor(
                    NS[:, 3:4], NS[:, 3:4], SCC[:, 8:9], ALU.subtract),
                    waits=[("vec", pv)], inc="vec")
            mt = S.op("tensor", lambda e, t=t: e.transpose(
                PM[:, :], M8[:, t * 8:(t + 1) * 8], IDF32[:, :]),
                waits=[("vec", S.c("vec"))], inc="mm")
            S.op("vector", lambda e, t=t: e.tensor_copy(
                M8F[:, t * 128:(t + 1) * 128], PM[:, :]),
                waits=[("mm", mt)], inc="vec")
        L.done()

    def l0_ffn():
        S.phase("l0ffn")
        L = Locals()
        PHa = L.ps("ph0a", [128, 512], F32)
        PHb = L.ps("ph0b", [128, 512], F32)
        PYa = L.ps("py0a", [128, 1024], F32)
        PYb = L.ps("py0b", [128, 1024], F32)
        PB1 = L.ps("pb1", [128, 128], F32)
        B2ALL = L.sb("B2ALL", [8, 1024], BF16)
        B1RAW = L.sb("B1RAW", [32, 128], F32)
        B1C = L.sb("B1C", [128, 64], F32)
        M8F = WS[0:8, 0:2048]
        ldb2 = S.dma("gpsimd", B2ALL[:, :], dram["L0b2"], "d_ld")
        for ts in range(NT):
            P = (PYa, PYb)[ts % 2]
            last = None
            for seg in range(2):
                last = S.op("tensor", lambda e, ts=ts, seg=seg, P=P: e.matmul(
                    P[:, seg * 512:(seg + 1) * 512],
                    M8F[:, ts * 128:(ts + 1) * 128],
                    B2ALL[:, seg * 512:(seg + 1) * 512],
                    start=True, stop=True),
                    waits=([ldb2] if (ts == 0 and seg == 0) else []) +
                          ([("vec", S.c("vec"))] if (seg == 0 and ts >= 2)
                           else []),
                    inc=("mm" if seg == 1 else None))
            S.op("vector", lambda e, ts=ts, P=P: e.tensor_copy(
                B3[:, ts * 1024:(ts + 1) * 1024], P[:, :]),
                waits=[("mm", last)], inc="vec")
        w1b = [B0[:, 0:8192], B0[:, 8192:16384]]
        w2b = [B1[:, 0:8192], B1[:, 8192:16384]]
        hb = [WS[:, 0:4096], WS[:, 4096:8192]]
        ld = {}
        mm_after_h = {}
        mm_after_y = {}

        def issue_loads(e_, q_, slot):
            key_prev = (e_ * 4 + q_) - 2
            w1w = []
            w2w = []
            if key_prev >= 0:
                w1w = [("mm", mm_after_h[key_prev])]
                w2w = [("mm", mm_after_y[(key_prev, 3)])]
            s1 = dram["L0w1"][e_][:, q_ * 1024:(q_ + 1) * 1024]
            s1 = s1.rearrange("(i p) n -> p i n", p=128)
            d1 = w1b[slot].rearrange("p (i n) -> p i n", n=1024)
            v1 = S.dma("gpsimd", d1, s1, "d_w1", waits=w1w)
            s2 = dram["L0w2"][e_][q_ * 1024:(q_ + 1) * 1024, :]
            s2 = s2.rearrange("(f p) n -> p f n", p=128)
            d2 = w2b[slot].rearrange("p (f n) -> p f n", n=1024)
            v2 = S.dma("gpsimd", d2, s2, "d_w2", waits=w2w)
            ld[(e_, q_)] = (v1, v2)

        issue_loads(0, 0, 0)
        issue_loads(0, 1, 1)
        for e_ in range(NE):
            lb1 = S.dma("sync", B1RAW[:, :], dram["L0b1"][e_], "d_ld",
                        waits=([("act", S.c("act"))] if e_ > 0 else ()))
            bt = S.op("tensor", lambda e: e.transpose(
                PB1[:, 0:32], B1RAW[:, :], IDF32[0:32, 0:32]),
                waits=[lb1] +
                      ([("vec", S.c("vec"))] if e_ > 0 else []), inc="mm")
            S.op("vector", lambda e: e.tensor_copy(B1C[:, 0:32], PB1[:, 0:32]),
                 waits=[("mm", bt)] +
                       ([("act", S.c("act"))] if e_ > 0 else []), inc="vec")
            b1v = S.c("vec")
            for q_ in range(4):
                key = e_ * 4 + q_
                slot = key % 2
                v1, v2 = ld[(e_, q_)]
                for tc in range(4):
                    hslot = tc % 2
                    hw = []
                    if key >= 1 or tc >= 2:
                        pk, ptc = (key, tc - 2) if tc >= 2 else (key - 1, tc + 2)
                        if (pk, ptc) in mm_after_y:
                            hw = [("mm", mm_after_y[(pk, ptc)])]
                    for fs in range(8):
                        PH = (PHa, PHb)[fs % 2]
                        last = None
                        for i in range(NDC):
                            w_ = []
                            if i == 0:
                                w_ += [("act", S.c("act"))]
                                if fs == 0:
                                    w_ += [v1] + hw
                            last = S.op("tensor",
                                        lambda e, fs=fs, tc=tc, i=i, slot=slot,
                                        PH=PH: e.matmul(
                                PH[:, :],
                                w1b[slot][:, i * 1024 + fs * 128:
                                          i * 1024 + (fs + 1) * 128],
                                B2[:, i * 2048 + tc * 512:i * 2048 + (tc + 1) * 512],
                                start=(i == 0), stop=(i == NDC - 1)),
                                waits=w_, inc=("mm" if i == NDC - 1 else None))
                            if fs == 7 and i == NDC - 1:
                                mm_after_h[key] = S.c("mm")
                        S.op("scalar", lambda e, fs=fs, q_=q_, hslot=hslot,
                             PH=PH, last=last: e.activation(
                                 hb[hslot][:, fs * 512:(fs + 1) * 512], PH[:, :],
                                 AF.Gelu, bias=B1C[:, q_ * 8 + fs:q_ * 8 + fs + 1]),
                             waits=[("mm", last), ("vec", b1v)], inc="act")
                    hact = S.c("act")
                    for tsl in range(4):
                        ts = tc * 4 + tsl
                        PY = (PYa, PYb)[ts % 2]
                        last = None
                        for fs in range(8):
                            for seg in range(2):
                                w_ = []
                                if fs == 0 and seg == 0:
                                    w_ = [("act", hact), v2,
                                          ("vec", S.c("vec"))]
                                last = S.op("tensor",
                                            lambda e, fs=fs, seg=seg, tsl=tsl,
                                            slot=slot, hslot=hslot, PY=PY:
                                            e.matmul(
                                    PY[:, seg * 512:(seg + 1) * 512],
                                    hb[hslot][:, fs * 512 + tsl * 128:
                                              fs * 512 + (tsl + 1) * 128],
                                    w2b[slot][:, fs * 1024 + seg * 512:
                                              fs * 1024 + (seg + 1) * 512],
                                    start=(fs == 0), stop=(fs == 7)),
                                    waits=w_,
                                    inc=("mm" if (fs == 7 and seg == 1)
                                         else None))
                        mm_after_y[(key, tc)] = last
                        S.op("vector", lambda e, ts=ts, e_=e_, PY=PY, last=last:
                             e.scalar_tensor_tensor(
                                 B3[:, ts * 1024:(ts + 1) * 1024], PY[:, :],
                                 M8[:, ts * 8 + e_:ts * 8 + e_ + 1],
                                 B3[:, ts * 1024:(ts + 1) * 1024],
                                 ALU.mult, ALU.add),
                             waits=[("mm", last)], inc="vec")
                nkey = key + 2
                if nkey < NE * 4:
                    issue_loads(nkey // 4, nkey % 4, nkey % 2)
        L.done()

    def l0_combine():
        S.phase("l0cmb")
        st_prev = ("d_st@sync", 0)
        for t in range(NT):
            sstage = (XTA, XTB)[t % 2]
            lds = S.dma("sync", sstage[:, :], S0[t * 128:(t + 1) * 128, :],
                        "d_ld", waits=([("vec", S.c("vec"))] if t >= 2 else ()))
            xstage = (SCA, SCB)[t % 2]
            ldx = S.dma("scalar", xstage[:, :],
                        dram["x"][t * 128:(t + 1) * 128, :], "d_ld")
            aa = S.op("scalar", lambda e, t=t: e.activation(
                WOB[:, :], B3[:, t * 1024:(t + 1) * 1024], AF.Silu),
                waits=[("vec", S.c("vec")), ("act", S.c("act"))], inc="act")
            vv = S.op("vector", lambda e, s=sstage: e.tensor_tensor(
                SCC[:, 0:1024], s[:, :], WOB[:, :], ALU.mult),
                waits=[("act", aa), lds], inc="vec")
            vv = S.op("vector", lambda e, xs=xstage: e.tensor_tensor(
                SCC[:, 1024:2048], xs[:, :], SCC[:, 0:1024], ALU.add),
                waits=[ldx, st_prev], inc="vec")
            st_prev = S.dma("sync", XRES[t * 128:(t + 1) * 128, :],
                            SCC[:, 1024:2048], "d_st", waits=[("vec", vv)])
        S.op("sync", lambda e: e.nop(), waits=S.dtok("d_st"))

    def l1_retention():
        vtm_phase(B2, dram["L1WvT"], B0, False, "l1v")
        S.phase("l1qk")
        with nc.psum_tensor("pl1qa", [128, 512], F32) as PA, \
             nc.psum_tensor("pl1qb", [128, 512], F32) as PB:
            for c in range(NDC):
                qk_chunk(B2, dram["L1WqT"], c, B3[:, c * 2048:(c + 1) * 2048],
                         PA, PB, scale=0.125, wslot=0)
                qk_chunk(B2, dram["L1WkT"], c, B1[:, c * 2048:(c + 1) * 2048],
                         PA, PB, wslot=1024)
        S.phase("l1ret")
        L = Locals()
        PSA = L.ps("pretS", [128, 2048], F32)
        POUT = L.ps("pretO", [128, 2048], F32)
        UROWs = L.sb("urow_l", [128, T], BF16)
        CTRIs = L.sb("ctri_l", [128, 128], BF16)
        CROW = L.sb("crow_l", [128, T], BF16)
        LNG = L.sb("lng_l", [128, 16], F32)
        GROW = L.sb("grow_l", [1, 32], F32)
        WTMP = L.sb("wtmp_l", [128, 2048], BF16)
        WSB = L.sb("wsb_l", [128, 2048], BF16)
        S.dma("sync", UROWs[:, :], dram["UROW"], "d_ld")
        S.dma("sync", CTRIs[:, :], dram["CTRI"], "d_ld")
        ldg = S.dma("sync", GROW[:, 0:16], dram["gammas"], "d_ld")
        aa = S.op("scalar", lambda e: e.activation(
            GROW[:, 16:32], GROW[:, 0:16], AF.Ln),
            waits=S.dtok("d_ld"), inc="act")
        lm = S.op("tensor", lambda e: e.matmul(
            PSA[:, 0:16], ONESROW[:, :], GROW[:, 16:32], start=True, stop=True),
            waits=[("act", aa)], inc="mm")
        S.op("vector", lambda e: e.tensor_copy(LNG[:, :], PSA[:, 0:16]),
             waits=[("mm", lm)], inc="vec")
        for H_ in range(NH):
            c, hh = H_ // 2, H_ % 2
            r0 = 64 * hh
            Q = B3[:, c * 2048:(c + 1) * 2048]
            K = B1[:, c * 2048:(c + 1) * 2048]
            pv = S.op("vector", lambda e, H_=H_: e.tensor_scalar(
                SM[:, 0:1], PIDX[:, :], LNG[:, H_:H_ + 1], None, ALU.mult),
                waits=[("vec", S.c("vec"))], inc="vec")
            aa = S.op("scalar", lambda e: e.activation(
                SM[:, 1:2], SM[:, 0:1], AF.Exp, scale=-1.0),
                waits=[("vec", pv)], inc="act")
            S.op("vector", lambda e, H_=H_: e.tensor_scalar(
                SCC[:, 0:2048], UROWs[:, :], LNG[:, H_:H_ + 1], None, ALU.mult),
                inc="vec")
            ca = S.op("scalar", lambda e: e.activation(
                CROW[:, :], SCC[:, 0:2048], AF.Exp),
                waits=[("vec", S.c("vec"))], inc="act")
            for j in range(NT):
                s0 = j * 128
                width = T - s0
                last = None
                for (alo, ahi) in pieces(0, width):
                    last = S.op("tensor", lambda e, j=j, alo=alo, ahi=ahi,
                                r0=r0, s0=s0, Q=Q, K=K: e.matmul(
                        PSA[:, alo:ahi], K[r0:r0 + 64, s0:s0 + 128],
                        Q[r0:r0 + 64, s0 + alo:s0 + ahi],
                        start=True, stop=True),
                        waits=([("vec", S.c("vec")), ("act", ca)]
                               if alo == 0 else []),
                        inc=("mm" if ahi == width else None))
                vv = S.op("vector", lambda e, width=width: e.tensor_scalar(
                    WTMP[:, 0:width], PSA[:, 0:width], SM[:, 1:2], None,
                    ALU.mult), waits=[("mm", last), ("act", aa)], inc="vec")
                vv = S.op("vector", lambda e, width=width: e.tensor_tensor(
                    WSB[:, 0:width], WTMP[:, 0:width], CROW[:, 0:width],
                    ALU.mult), inc="vec")
                vv = S.op("vector", lambda e: e.tensor_tensor(
                    WSB[:, 0:128], WSB[:, 0:128], CTRIs[:, :], ALU.mult),
                    inc="vec")
                for (alo, ahi) in pieces(s0, T):
                    bank = alo // 512
                    lastj = min(NT - 1, 4 * bank + 3)
                    S.op("tensor", lambda e, j=j, alo=alo, ahi=ahi, s0=s0,
                         H_=H_: e.matmul(
                        POUT[0:64, alo:ahi],
                        B0[:, j * 1024 + H_ * 64:j * 1024 + (H_ + 1) * 64],
                        WSB[:, alo - s0:ahi - s0],
                        start=(j == 0), stop=(j == lastj)),
                        waits=[("vec", vv)] if alo == s0 else [],
                        inc=("mm" if (j == NT - 1 and ahi == T) else None))
            mmv = S.c("mm")
            vv = S.op("vector", lambda e: e.tensor_copy(
                WTMP[0:64, :], POUT[0:64, :]), waits=[("mm", mmv)], inc="vec")
            dstrip = dram["asp"].rearrange("t r c -> r t c")[64 * hh + 128 * c:
                                                            64 * hh + 128 * c + 64]
            sstrip = WTMP[0:64, :].rearrange("p (t c) -> p t c", c=128)
            S.dma("sync", dstrip, sstrip, "d_st", waits=[("vec", vv)])
            S.op("vector", lambda e: e.engine_nop(),
                 waits=S.dtok("d_st"), inc="vec")
        L.done()
        wo_proj(("asp",), dram["L1WoT"], None, S1, "l1wo")

    def l1_softmax_attn():
        norm_phase(XN1, B2, None, "l1rms1", "rms", dram["ln1w"],
                   ntiles=rms_ntiles)
        vtm_phase(B2, dram["F1WvT"], B0, True, "f1v")
        S.phase("f1qk")
        with nc.psum_tensor("pf1qa", [128, 512], F32) as PA, \
             nc.psum_tensor("pf1qb", [128, 512], F32) as PB:
            for c in range(NDC):
                qk_chunk(B2, dram["F1WqT"], c, B3[:, c * 2048:(c + 1) * 2048],
                         PA, PB, scale=0.125, wslot=0)
                qk_chunk(B2, dram["F1WkT"], c, B1[:, c * 2048:(c + 1) * 2048],
                         PA, PB, wslot=1024)
        S.phase("f1sm")
        L = Locals()
        PSA = L.ps("psmS", [128, 2048], F32)
        POUT = L.ps("psmO", [128, 2048], F32)
        ROWS = L.sb("rows_l", [1, 4096], BF16)
        RNB = L.sb("rn_l", [1, 2048], BF16)
        RRF = L.sb("rr_l", [1, 2048], F32)
        NEGBS = L.sb("negbs_l", [128, 2048], BF16)
        PTS = L.sb("pts_l", [128, 2048], BF16)
        WTMP = PTS
        RECBS = NEGBS[0:64, :]
        Rq = ROWS[:, 0:2048]
        Rk = ROWS[:, 2048:4096]
        Rn = RNB[:, 0:2048]
        Rr = RRF[:, 0:2048]
        for H_ in range(NH):
            c, hh = H_ // 2, H_ % 2
            r0 = 64 * hh
            Q = B3[:, c * 2048:(c + 1) * 2048]
            K = B1[:, c * 2048:(c + 1) * 2048]
            vv = S.op("vector", lambda e, Q=Q, r0=r0: e.tensor_tensor(
                WTMP[0:64, :], Q[r0:r0 + 64, :], Q[r0:r0 + 64, :], ALU.mult),
                waits=[("mm", S.c("mm")), ("act", S.c("act"))], inc="vec")
            last = None
            for (alo, ahi) in pieces(0, T):
                last = S.op("tensor", lambda e, alo=alo, ahi=ahi: e.matmul(
                    PSA[0:1, alo:ahi], ONESCOL[0:64, :], WTMP[0:64, alo:ahi],
                    start=True, stop=True), waits=[("vec", vv)],
                    inc=("mm" if ahi == T else None))
            S.op("scalar", lambda e: e.activation(
                Rq, PSA[0:1, 0:2048], AF.Ln, bias=FB[0:1, 1:2]),
                waits=[("mm", last)])
            aq = S.op("scalar", lambda e: e.activation(
                Rq, Rq, AF.Exp, scale=0.5), inc="act")
            vv = S.op("vector", lambda e, K=K, r0=r0: e.tensor_tensor(
                WTMP[0:64, :], K[r0:r0 + 64, :], K[r0:r0 + 64, :], ALU.mult),
                waits=[("act", aq)], inc="vec")
            last = None
            for (alo, ahi) in pieces(0, T):
                last = S.op("tensor", lambda e, alo=alo, ahi=ahi: e.matmul(
                    PSA[0:1, alo:ahi], ONESCOL[0:64, :], WTMP[0:64, alo:ahi],
                    start=True, stop=True), waits=[("vec", vv)],
                    inc=("mm" if ahi == T else None))
            S.op("scalar", lambda e: e.activation(
                Rk, PSA[0:1, 0:2048], AF.Ln, bias=FB[0:1, 1:2]),
                waits=[("mm", last)])
            ak = S.op("scalar", lambda e: e.activation(
                Rk, Rk, AF.Exp, scale=0.5), inc="act")
            vv = S.op("vector", lambda e: e.reduce_max(
                SM[0:1, 0:1], Rk, axis=AX.X), waits=[("act", ak)], inc="vec")
            vv = S.op("vector", lambda e: e.tensor_scalar(
                Rn, Rq, SM[0:1, 0:1], -1.0, ALU.mult, ALU.mult),
                waits=[("vec", vv)], inc="vec")
            last = None
            for (alo, ahi) in pieces(0, T):
                last = S.op("tensor", lambda e, alo=alo, ahi=ahi: e.matmul(
                    POUT[:, alo:ahi], ONESROWB[:, :], Rn[:, alo:ahi],
                    start=True, stop=True), waits=[("vec", vv)],
                    inc=("mm" if ahi == T else None))
            nb = S.op("vector", lambda e: e.tensor_copy(NEGBS[:, :], POUT[:, :]),
                      waits=[("mm", last)], inc="vec")
            for j in range(NT):
                last = None
                for (alo, ahi) in pieces(0, T):
                    last = S.op("tensor", lambda e, j=j, alo=alo, ahi=ahi,
                                r0=r0, Q=Q, K=K: e.matmul(
                        PSA[:, alo:ahi], K[r0:r0 + 64, j * 128:(j + 1) * 128],
                        Q[r0:r0 + 64, alo:ahi], start=True, stop=True),
                        waits=([("act", S.c("act")), ("vec", nb)]
                               if alo == 0 else []),
                        inc=("mm" if ahi == T else None))
                vv = S.op("vector", lambda e: e.tensor_tensor(
                    SCC[:, :], PSA[:, :], NEGBS[:, :], ALU.add),
                    waits=[("mm", last)], inc="vec")
                pa = S.op("scalar", lambda e: e.activation(
                    PTS[:, :], SCC[:, :], AF.Exp), waits=[("vec", vv)],
                    inc="act")
                for (alo, ahi) in pieces(0, T):
                    S.op("tensor", lambda e, j=j, alo=alo, ahi=ahi, H_=H_:
                         e.matmul(
                        POUT[0:65, alo:ahi],
                        B0[:, j * 1040 + H_ * 65:j * 1040 + (H_ + 1) * 65],
                        PTS[:, alo:ahi], start=(j == 0), stop=(j == NT - 1)),
                        waits=[("act", pa)] if alo == 0 else [],
                        inc=("mm" if (j == NT - 1 and ahi == T) else None))
            mmv = S.c("mm")
            vv = S.op("vector", lambda e: e.reciprocal(Rr, POUT[64:65, 0:2048]),
                      waits=[("mm", mmv)], inc="vec")
            last = None
            for (alo, ahi) in pieces(0, T):
                last = S.op("tensor", lambda e, alo=alo, ahi=ahi: e.matmul(
                    PSA[0:64, alo:ahi], ONESROW[0:1, 0:64], Rr[:, alo:ahi],
                    start=True, stop=True), waits=[("vec", vv)],
                    inc=("mm" if ahi == T else None))
            vv = S.op("vector", lambda e: e.tensor_copy(RECBS[:, :],
                                                        PSA[0:64, :]),
                      waits=[("mm", last)], inc="vec")
            vv = S.op("vector", lambda e: e.tensor_tensor(
                PTS[0:64, :], POUT[0:64, :], RECBS[:, :], ALU.mult), inc="vec")
            dstrip = dram["asp"].rearrange("t r c -> r t c")[64 * hh + 128 * c:
                                                            64 * hh + 128 * c + 64]
            sstrip = PTS[0:64, :].rearrange("p (t c) -> p t c", c=128)
            S.dma("sync", dstrip, sstrip, "d_st", waits=[("vec", vv)])
            S.op("vector", lambda e: e.engine_nop(),
                 waits=S.dtok("d_st"), inc="vec")
        L.done()
        wo_proj(("asp",), dram["F1WoT"], XN1, H1, "f1wo")

    def l1_router():
        S.phase("l1rt")
        L = Locals()
        PL = L.ps("plog1", [128, 512], F32)
        GW = L.sb("GWsb1", [128, 64], BF16)
        LGT = L.sb("LGT1", [128, 8], F32)
        L2T = L.sb("L2T1", [128, 8], F32)
        EQ = L.sb("EQsb1", [128, 8], F32)
        M1H = L.sb("M1H", [128, 8], F32)
        M2H = L.sb("M2H", [128, 8], F32)
        NS = L.sb("NSsb1", [128, 8], F32)
        src = dram["F1gwT"].rearrange("(c p) n -> p c n", p=128)
        ldv = S.dma("gpsimd", GW[:, :].rearrange("p (c n) -> p c n", n=8),
                    src, "d_ld")

        def firsthot(dst, eqb, t):
            pv = S.op("vector", lambda e: e.memset(NS[:, 3:4], 1.0),
                      waits=[("vec", S.c("vec"))], inc="vec")
            for ei in range(NE):
                pv = S.op("vector", lambda e, ei=ei, eqb=eqb: e.tensor_tensor(
                    SCC[:, 8:9], eqb[:, ei:ei + 1], NS[:, 3:4], ALU.mult),
                    waits=[("vec", pv)], inc="vec")
                pv = S.op("vector", lambda e, ei=ei, dst=dst: e.tensor_copy(
                    dst[:, ei:ei + 1], SCC[:, 8:9]),
                    waits=[("vec", pv)], inc="vec")
                pv = S.op("vector", lambda e: e.tensor_tensor(
                    NS[:, 3:4], NS[:, 3:4], SCC[:, 8:9], ALU.subtract),
                    waits=[("vec", pv)], inc="vec")

        for t in range(NT):
            last = None
            for i in range(NDC):
                last = S.op("tensor", lambda e, t=t, i=i: e.matmul(
                    PL[:, 0:8], B2[:, i * 2048 + t * 128:i * 2048 + (t + 1) * 128],
                    GW[:, i * 8:(i + 1) * 8], start=(i == 0),
                    stop=(i == NDC - 1)),
                    waits=([ldv] if (t == 0 and i == 0) else []) +
                          ([("vec", S.c("vec"))] if i == 0 else []),
                    inc=("mm" if i == NDC - 1 else None))
            lv = S.op("vector", lambda e: e.tensor_copy(LGT[:, :], PL[:, 0:8]),
                      waits=[("mm", last)], inc="vec")
            r1 = S.op("vector", lambda e: e.reduce_max(
                NS[:, 0:1], LGT[:, :], axis=AX.X, negate=True),
                waits=[("vec", lv)], inc="vec")
            S.op("vector", lambda e: e.tensor_scalar(
                EQ[:, :], LGT[:, :], NS[:, 0:1], 0.0, ALU.add, ALU.is_ge),
                waits=[("vec", r1)])
            firsthot(M1H, EQ, t)
            pv = S.op("vector", lambda e: e.tensor_scalar(
                SCC[:, 16:24], M1H[:, :], 1e9, None, ALU.mult),
                waits=[("vec", S.c("vec"))], inc="vec")
            pv = S.op("vector", lambda e: e.tensor_tensor(
                L2T[:, :], LGT[:, :], SCC[:, 16:24], ALU.subtract),
                waits=[("vec", pv)], inc="vec")
            r2 = S.op("vector", lambda e: e.reduce_max(
                NS[:, 1:2], L2T[:, :], axis=AX.X, negate=True),
                waits=[("vec", pv)], inc="vec")
            S.op("vector", lambda e: e.tensor_scalar(
                EQ[:, :], L2T[:, :], NS[:, 1:2], 0.0, ALU.add, ALU.is_ge),
                waits=[("vec", r2)])
            firsthot(M2H, EQ, t)
            vv = S.op("vector", lambda e: e.tensor_tensor(
                NS[:, 4:5], NS[:, 1:2], NS[:, 0:1], ALU.subtract),
                waits=[("vec", S.c("vec"))], inc="vec")
            aa = S.op("scalar", lambda e: e.activation(
                NS[:, 5:6], NS[:, 4:5], AF.Exp, scale=-1.0),
                waits=[("vec", vv)], inc="act")
            c1 = S.op("vector", lambda e: e.tensor_scalar(
                NS[:, 6:7], NS[:, 5:6], 1.0, None, ALU.add),
                waits=[("act", aa)], inc="vec")
            c2 = S.op("vector", lambda e: e.reciprocal(NS[:, 7:8], NS[:, 6:7]),
                      waits=[("vec", c1)], inc="vec")
            c3 = S.op("vector", lambda e, t=t: e.tensor_scalar(
                NS[:, 5:6], NS[:, 7:8], -1.0, 1.0, ALU.mult, ALU.add),
                waits=[("vec", c2)], inc="vec")
            c4 = S.op("vector", lambda e, t=t: e.tensor_scalar(
                SCC[:, 16:24], M2H[:, :], NS[:, 5:6], None, ALU.mult),
                waits=[("vec", c3)], inc="vec")
            S.op("vector", lambda e, t=t: e.scalar_tensor_tensor(
                M8[:, t * 8:(t + 1) * 8], M1H[:, :], NS[:, 7:8],
                SCC[:, 16:24], ALU.mult, ALU.add),
                waits=[("vec", c4)], inc="vec")
            if DUMP_SBUF:
                for off, srcb in ((0, LGT), (8, M1H), (16, M2H), (24, NS),
                                  (32, L2T)):
                    S.op("vector", lambda e, t=t, off=off, srcb=srcb:
                         e.tensor_copy(B0[:, t * 64 + off:t * 64 + off + 8],
                                       srcb[:, 0:8]), inc="vec")
        L.done()

    def l1_moe():
        norm_phase(H1, B2, None, "l1rms2", "rms", dram["ln2w"])
        l1_router()
        S.phase("l1moe")
        L = Locals()
        PHa = L.ps("ph1a", [128, 512], F32)
        PHb = L.ps("ph1b", [128, 512], F32)
        PYa = L.ps("py1a", [128, 1024], F32)
        PYb = L.ps("py1b", [128, 1024], F32)
        w1b = [B0[:, 0:8192], B0[:, 8192:16384]]
        w3b = [B1[:, 0:8192], B1[:, 8192:16384]]
        w2b = [B3[:, 0:8192], B3[:, 8192:16384]]
        hb = [WS[:, 0:4096], WS[:, 4096:8192]]
        ld = {}
        mm_after_h = {}
        mm_after_y = {}
        gd_prev = {}

        def issue_loads(key, slot):
            e_, hf = key // 2, key % 2
            w1w = []
            w2w = []
            if key - 2 >= 0:
                w1w = [("mm", mm_after_h[key - 2])]
                w2w = [("mm", mm_after_y[(key - 2, 3)])]
            s1 = dram["F1w1T"][e_][:, hf * 1024:(hf + 1) * 1024]
            s1 = s1.rearrange("(i p) n -> p i n", p=128)
            v1 = S.dma("gpsimd", w1b[slot].rearrange("p (i n) -> p i n", n=1024),
                       s1, "d_w1", waits=w1w)
            s3 = dram["F1w3T"][e_][:, hf * 1024:(hf + 1) * 1024]
            s3 = s3.rearrange("(i p) n -> p i n", p=128)
            v3 = S.dma("gpsimd", w3b[slot].rearrange("p (i n) -> p i n", n=1024),
                       s3, "d_w1", waits=w1w)
            s2 = dram["F1w2T"][e_][hf * 1024:(hf + 1) * 1024, :]
            s2 = s2.rearrange("(f p) n -> p f n", p=128)
            v2 = S.dma("gpsimd", w2b[slot].rearrange("p (f n) -> p f n", n=1024),
                       s2, "d_w2", waits=w2w)
            ld[key] = (v1, v3, v2)

        issue_loads(0, 0)
        issue_loads(1, 1)
        for key in range(NE * 2):
            e_, hf = key // 2, key % 2
            slot = key % 2
            v1, v3, v2 = ld[key]
            for tc in range(4):
                hslot = tc % 2
                hw = []
                pk, ptc = (key, tc - 2) if tc >= 2 else (key - 1, tc + 2)
                if (pk, ptc) in mm_after_y:
                    hw = [("mm", mm_after_y[(pk, ptc)])]
                for fs in range(8):
                    PH = (PHa, PHb)[fs % 2]
                    last = None
                    for i in range(NDC):
                        w_ = []
                        if i == 0:
                            w_ += [("act", S.c("act")), ("vec", S.c("vec"))]
                            if fs == 0:
                                w_ += [v1, v3] + hw
                        last = S.op("tensor", lambda e, fs=fs, tc=tc, i=i,
                                    slot=slot, PH=PH: e.matmul(
                            PH[:, :],
                            w1b[slot][:, i * 1024 + fs * 128:
                                      i * 1024 + (fs + 1) * 128],
                            B2[:, i * 2048 + tc * 512:i * 2048 + (tc + 1) * 512],
                            start=(i == 0), stop=(i == NDC - 1)),
                            waits=w_, inc=("mm" if i == NDC - 1 else None))
                    aa = S.op("scalar", lambda e, PH=PH: e.activation(
                        SCC[:, 0:512], PH[:, :], AF.Silu),
                        waits=[("mm", last)], inc="act")
                    PH3 = (PHa, PHb)[(fs + 1) % 2]
                    last = None
                    for i in range(NDC):
                        last = S.op("tensor", lambda e, fs=fs, tc=tc, i=i,
                                    slot=slot, PH3=PH3: e.matmul(
                            PH3[:, :],
                            w3b[slot][:, i * 1024 + fs * 128:
                                      i * 1024 + (fs + 1) * 128],
                            B2[:, i * 2048 + tc * 512:i * 2048 + (tc + 1) * 512],
                            start=(i == 0), stop=(i == NDC - 1)),
                            waits=([("act", S.c("act"))] if i == 0 else []),
                            inc=("mm" if i == NDC - 1 else None))
                        if i == NDC - 1 and fs == 7:
                            mm_after_h[key] = S.c("mm")
                    S.op("vector", lambda e, fs=fs, hslot=hslot, PH3=PH3,
                         last=last, aa=aa: e.tensor_tensor(
                             hb[hslot][:, fs * 512:(fs + 1) * 512], PH3[:, :],
                             SCC[:, 0:512], ALU.mult),
                         waits=[("mm", last), ("act", aa)], inc="vec")
                hvec = S.c("vec")
                for tsl in range(4):
                    ts = tc * 4 + tsl
                    PY = (PYa, PYb)[ts % 2]
                    last = None
                    for fs in range(8):
                        for seg in range(2):
                            w_ = []
                            if fs == 0 and seg == 0:
                                w_ = [("vec", hvec), v2,
                                      ("vec", S.c("vec"))]
                            last = S.op("tensor", lambda e, fs=fs, seg=seg,
                                        tsl=tsl, slot=slot, hslot=hslot, PY=PY:
                                        e.matmul(
                                PY[:, seg * 512:(seg + 1) * 512],
                                hb[hslot][:, fs * 512 + tsl * 128:
                                          fs * 512 + (tsl + 1) * 128],
                                w2b[slot][:, fs * 1024 + seg * 512:
                                          fs * 1024 + (seg + 1) * 512],
                                start=(fs == 0), stop=(fs == 7)),
                                waits=w_,
                                inc=("mm" if (fs == 7 and seg == 1) else None))
                    mm_after_y[(key, tc)] = last
                    stg = (SCA, SCB)[ts % 2]
                    vv = S.op("vector", lambda e, ts=ts, e_=e_, PY=PY, stg=stg:
                              e.tensor_scalar(
                                  stg[:, :], PY[:, :],
                                  M8[:, ts * 8 + e_:ts * 8 + e_ + 1], None,
                                  ALU.mult),
                              waits=[("mm", last),
                                     gd_prev.get(ts % 2, ("d_gd@g0", 0))],
                              inc="vec")
                    gd_prev[ts % 2] = (f"d_gd@g{ts % 2}", S.op(
                        "gpsimd", lambda e, ts=ts, stg=stg, key=key:
                        e.dma_start(GD[ts * 128:(ts + 1) * 128, :], stg[:, :],
                                    accum_op=(ALU.bypass if key == 0
                                              else ALU.add)),
                        waits=[("vec", vv)], inc=f"d_gd@g{ts % 2}", inc_amt=16))
            if key + 2 < NE * 2:
                issue_loads(key + 2, (key + 2) % 2)
        L.done()

    def l1_combine():
        S.phase("l1cmb")
        st_prev = ("d_st@sync", 0)
        for t in range(NT):
            h1s = (XTA, XTB)[t % 2]
            ldh = S.dma("sync", h1s[:, :], H1[t * 128:(t + 1) * 128, :], "d_ld",
                        waits=([("vec", S.c("vec"))] if t >= 2 else ()))
            gs = (SCA, SCB)[t % 2]
            ldg = S.dma("scalar", gs[:, :], GD[t * 128:(t + 1) * 128, :], "d_ld",
                        waits=S.dtok("d_gd") if t == 0 else ())
            lds = S.dma("sync", ADD[:, :], S1[t * 128:(t + 1) * 128, :], "d_ld",
                        waits=([("vec", S.c("vec"))] if t >= 1 else ()))
            vv = S.op("vector", lambda e, h1s=h1s, gs=gs: e.tensor_tensor(
                SCC[:, 0:1024], h1s[:, :], gs[:, :], ALU.add),
                waits=S.dtok("d_ld"), inc="vec")
            aa = S.op("scalar", lambda e: e.activation(
                WOB[:, :], SCC[:, 0:1024], AF.Silu), waits=[("vec", vv)],
                inc="act")
            vv = S.op("vector", lambda e: e.tensor_tensor(
                SCC[:, 0:1024], ADD[:, :], WOB[:, :], ALU.mult),
                waits=[("act", aa)], inc="vec")
            ldx = S.dma("scalar", gs[:, :], XRES[t * 128:(t + 1) * 128, :],
                        "d_ld", waits=[("vec", vv)])
            vv = S.op("vector", lambda e, gs=gs: e.tensor_tensor(
                SCC[:, 1024:2048], gs[:, :], SCC[:, 0:1024], ALU.add),
                waits=[ldx, st_prev], inc="vec")
            st_prev = S.dma("sync", dram["y"][t * 128:(t + 1) * 128, :],
                            SCC[:, 1024:2048], "d_st", waits=[("vec", vv)])
        S.op("sync", lambda e: e.nop(), waits=S.dtok("d_st"))

    load_consts()
    if 0 in layers:
        norm_phase(dram["x"], B2, None, "ln0", "ln")
        l0_attn()
        l0_router()
        l0_ffn()
        l0_combine()
    if 1 in layers:
        norm_phase(XRES, B2, (None if "xn1store" in SKIP else XN1), "ln1", "ln")
        if "ret" not in SKIP:
            l1_retention()
        l1_softmax_attn()
        if "moe" not in SKIP:
            l1_moe()
        if "cmb" not in SKIP:
            l1_combine()

    if stop_after is not None:
        idx = max(i for i, lab in S.labels if lab == stop_after)
        S.phases = S.phases[:idx + 1]
        if DUMP_SBUF:
            S.phases.append([])
            for nm, buf, w in (("dbgB0", B0, 16640), ("dbgB1", B1, 16384),
                               ("dbgB2", B2, 16384), ("dbgB3", B3, 16384),
                               ("dbgWS", WS, 8192)):
                d = nc.dram_tensor(nm, [128, w], BF16, kind="ExternalOutput").ap()
                S.dma("sync", d, buf[:, 0:w], "d_dbg")
            d = nc.dram_tensor("dbgM8", [128, 128], F32,
                               kind="ExternalOutput").ap()
            S.dma("sync", d, M8[:, :], "d_dbg")
            S.op("sync", lambda e: e.nop(), waits=S.dtok("d_dbg"))
    S.emit()
    return nc, dram


def _prep_shared(inputs, layers):
    f = np.float32
    sh = {}

    def tr(a):
        return np.ascontiguousarray(np.asarray(a, dtype=f).T)

    if 0 in layers:
        sh["L0WqT"] = tr(inputs["L0_attn_Wq"])
        sh["L0WkT"] = tr(inputs["L0_attn_Wk"])
        sh["L0WvT"] = tr(inputs["L0_attn_Wv"])
        sh["L0WoT"] = tr(inputs["L0_attn_Wo"])
        sh["L0WfT"] = tr(inputs["L0_attn_Wf"])
        sh["L0gw"] = np.ascontiguousarray(np.asarray(inputs["L0_ffn_gate_w"],
                                                     dtype=f))
        sh["L0w1"] = np.ascontiguousarray(np.asarray(inputs["L0_ffn_w1"],
                                                     dtype=f))
        sh["L0b1"] = np.ascontiguousarray(
            np.asarray(inputs["L0_ffn_b1"], dtype=f).reshape(NE, 32, 128))
        sh["L0w2"] = np.ascontiguousarray(np.asarray(inputs["L0_ffn_w2"],
                                                     dtype=f))
        sh["L0b2"] = np.ascontiguousarray(np.asarray(inputs["L0_ffn_b2"],
                                                     dtype=f))
    if 1 in layers:
        sh["L1WqT"] = tr(inputs["L1_attn_wq"])
        sh["L1WkT"] = tr(inputs["L1_attn_wk"])
        sh["L1WvT"] = tr(inputs["L1_attn_wv"])
        sh["L1WoT"] = tr(inputs["L1_attn_wo"])
        sh["gammas"] = np.ascontiguousarray(
            np.asarray(inputs["L1_attn_gammas"], dtype=f).reshape(1, NH))
        sh["ln1w"] = np.ascontiguousarray(
            np.asarray(inputs["L1_ffn_ln1_w"], dtype=f).reshape(1, D))
        sh["ln2w"] = np.ascontiguousarray(
            np.asarray(inputs["L1_ffn_ln2_w"], dtype=f).reshape(1, D))
        sh["F1WqT"] = tr(inputs["L1_ffn_wq"])
        sh["F1WkT"] = tr(inputs["L1_ffn_wk"])
        sh["F1WvT"] = tr(inputs["L1_ffn_wv"])
        sh["F1WoT"] = tr(inputs["L1_ffn_wo"])
        sh["F1gwT"] = tr(inputs["L1_ffn_gate_w"])
        sh["F1w1T"] = np.ascontiguousarray(
            np.asarray(inputs["L1_ffn_w1"], dtype=f).transpose(0, 2, 1))
        sh["F1w3T"] = np.ascontiguousarray(
            np.asarray(inputs["L1_ffn_w3"], dtype=f).transpose(0, 2, 1))
        sh["F1w2T"] = np.ascontiguousarray(
            np.asarray(inputs["L1_ffn_w2"], dtype=f).transpose(0, 2, 1))
    sh["IDBF"] = np.eye(128, dtype=BF)
    sh["IDF32"] = np.eye(128, dtype=f)
    sh["ONESROW"] = np.ones((1, 128), dtype=f)
    sh["ONESROWB"] = np.ones((1, 128), dtype=BF)
    sh["ONESCOL"] = np.ones((128, 1), dtype=BF)
    sh["PIDX"] = np.arange(128, dtype=f).reshape(128, 1)
    sh["UROW"] = np.broadcast_to(np.arange(T, dtype=f), (128, T)).astype(BF)
    sh["UROW"] = np.ascontiguousarray(sh["UROW"])
    ct = (np.arange(128)[None, :] >= np.arange(128)[:, None])
    sh["CTRI"] = np.ascontiguousarray(ct.astype(BF))
    return sh


def kernel(**inputs):
    x = np.ascontiguousarray(np.asarray(inputs["L0_attn_hidden_states"],
                                        dtype=np.float32))
    ncores = 8
    if not UNFUSED:
        sh = _prep_shared(inputs, (0, 1))
        nc, _ = build_core_program((0, 1))
        in_maps = [dict(sh, x=np.ascontiguousarray(x[b]))
                   for b in range(ncores)]
        res = run_bass_kernel_spmd(nc, in_maps, core_ids=list(range(ncores)))
        return np.stack([np.asarray(res.results[b]["y"], dtype=np.float32)
                         for b in range(ncores)], axis=0)
    sh0 = _prep_shared(inputs, (0,))
    nc0, _ = build_core_program((0,))
    maps0 = [dict(sh0, x=np.ascontiguousarray(x[b])) for b in range(ncores)]
    r0 = run_bass_kernel_spmd(nc0, maps0, core_ids=list(range(ncores)))
    x1 = [np.asarray(r0.results[b]["xout"], dtype=np.float32)
          for b in range(ncores)]
    sh1 = _prep_shared(inputs, (1,))
    nc1, _ = build_core_program((1,))
    maps1 = [dict(sh1, xin1=np.ascontiguousarray(x1[b]))
             for b in range(ncores)]
    r1 = run_bass_kernel_spmd(nc1, maps1, core_ids=list(range(ncores)))
    return np.stack([np.asarray(r1.results[b]["y"], dtype=np.float32)
                     for b in range(ncores)], axis=0)


# version 03 (final)
# speedup vs baseline: 1.0165x; 1.0165x over previous
import numpy as np
import ml_dtypes
import concourse.bass as bass
import concourse.mybir as mybir
from concourse.bass_utils import run_bass_kernel_spmd

F32 = mybir.dt.float32
BF16 = mybir.dt.bfloat16
AF = mybir.ActivationFunctionType
ALU = mybir.AluOpType
AX = mybir.AxisListType
BF = ml_dtypes.bfloat16

T = 2048
D = 1024
NT = 16
NDC = 8
NH = 16
HD = 64
FD = 256
DFF0 = 4096
NE = 8
DFF1 = 2048

ENGINES = ("sync", "scalar", "vector", "tensor", "gpsimd")

UNFUSED = False
DETECT_RACES = True
DUMP_SBUF = False
DBG_BAR = 0
SKIP = set()
L0A_STEPS = 99
L0A_MASK = None


def pieces(lo, hi, step=512):
    out = []
    a = lo
    while a < hi:
        b = min(hi, (a // step + 1) * step)
        out.append((a, b))
        a = b
    return out


class Sched:
    def __init__(self, nc):
        self.nc = nc
        self.phases = [[]]
        self.semh = {}
        self.semc = {}

    def sem(self, name):
        if name not in self.semh:
            self.semh[name] = self.nc.semaphore(name).__enter__()
            self.semc[name] = 0
        return name

    def c(self, name):
        self.sem(name)
        return self.semc[name]

    def phase(self, label=""):
        if self.phases[-1]:
            self.phases.append([])
        self.labels = getattr(self, "labels", [])
        self.labels.append((len(self.phases) - 1, label))

    def op(self, eng, fn, waits=(), inc=None, inc_amt=1):
        for (s, v) in waits:
            self.sem(s)
            assert v <= self.semc[s], f"wait on future {s}={v}>{self.semc[s]}"
        post = None
        if inc is not None:
            self.sem(inc)
            self.semc[inc] += inc_amt
            post = (inc, inc_amt)
        self.phases[-1].append(
            (eng, tuple((s, v) for s, v in waits if v > 0), fn, post))
        return self.semc[inc] if inc else None

    def dma(self, eng, out_ap, in_ap, sem, waits=()):
        self.rr = getattr(self, "rr", {})
        k = (sem, eng)
        i = self.rr.get(k, 0)
        self.rr[k] = i + 1
        s = f"{sem}@{eng}#{i % 8}"
        def fn(e, o=out_ap, i=in_ap):
            return e.dma_start(o, i)
        v = self.op(eng, fn, waits=waits, inc=self.sem(s), inc_amt=16)
        return (s, v)

    def dtok(self, sem):
        return [(s, v) for s, v in self.semc.items()
                if s.startswith(sem + "@") and v > 0]

    def emit(self):
        nc = self.nc
        for recs in self.phases:
            if not recs:
                continue
            with nc.Block() as block:
                for ename in ENGINES:
                    rs = [r for r in recs if r[0] == ename]
                    if rs:
                        self._emit_engine(block, ename, rs)

    def _emit_engine(self, block, ename, rs):
        semh = self.semh
        waited = {}

        def body(eng):
            for (_e, waits, fn, post) in rs:
                for (s, v) in waits:
                    if waited.get(s, -1) >= v:
                        continue
                    eng.wait_ge(semh[s], v)
                    waited[s] = v
                ins = fn(eng)
                if post is not None:
                    ins.then_inc(semh[post[0]], post[1])
        getattr(block, ename)(body)


def build_core_program(layers=(0, 1), taps=(), stop_after=None,
                       rms_ntiles=NT):
    nc = bass.Bass("TRN2", target_bir_lowering=False,
                   detect_race_conditions=DETECT_RACES)
    S = Sched(nc)
    taps = set(taps)
    dram = {}

    class Locals:

        def __init__(self):
            self.guards = []

        def sb(self, name, shape, dtype):
            g = nc.sbuf_tensor(name, list(shape), dtype)
            h = g.__enter__()
            self.guards.append(g)
            return h

        def ps(self, name, shape, dtype=F32):
            g = nc.psum_tensor(name, list(shape), dtype)
            h = g.__enter__()
            self.guards.append(g)
            return h

        def done(self):
            for g in reversed(self.guards):
                g.__exit__(None, None, None)
            self.guards = []

    def din(name, shape, dtype=F32):
        dram[name] = nc.dram_tensor(name, list(shape), dtype,
                                    kind="ExternalInput").ap()
        return dram[name]

    def dout(name, shape, dtype=F32):
        dram[name] = nc.dram_tensor(name, list(shape), dtype,
                                    kind="ExternalOutput").ap()
        return dram[name]

    def dint(name, shape, dtype):
        kind = "ExternalOutput" if name in taps else "Internal"
        dram[name] = nc.dram_tensor(name, list(shape), dtype, kind=kind).ap()
        return dram[name]

    if 0 in layers:
        din("x", (T, D))
        for n in ("L0WqT", "L0WkT", "L0WvT", "L0WoT"):
            din(n, (D, D))
        din("L0WfT", (HD, FD))
        din("L0gw", (D, NE))
        din("L0w1", (NE, D, DFF0))
        din("L0b1", (NE, 32, 128))
        din("L0w2", (NE, DFF0, D))
        din("L0b2", (NE, D))
    if 1 in layers:
        if 0 not in layers:
            din("xin1", (T, D))
        for n in ("L1WqT", "L1WkT", "L1WvT", "L1WoT",
                  "F1WqT", "F1WkT", "F1WvT", "F1WoT"):
            din(n, (D, D))
        din("gammas", (1, NH))
        din("ln1w", (1, D))
        din("ln2w", (1, D))
        din("F1gwT", (D, NE))
        din("F1w1T", (NE, D, DFF1))
        din("F1w3T", (NE, D, DFF1))
        din("F1w2T", (NE, DFF1, D))
    din("IDBF", (128, 128), BF16)
    din("IDF32", (128, 128), F32)
    din("ONESROW", (1, 128), F32)
    din("ONESROWB", (1, 128), BF16)
    din("ONESCOL", (128, 1), BF16)
    din("PIDX", (128, 1), F32)
    din("UROW", (128, T), BF16)
    din("CTRI", (128, 128), BF16)

    if layers == (0,):
        XRES = dout("xout", (T, D))
    elif layers == (1,):
        dout("y", (T, D))
        XRES = dram["xin1"]
    else:
        dout("y", (T, D))
        XRES = dint("xres", (T, D), F32)
    if 0 in layers:
        S0 = dint("s0", (T, D), BF16)
        ATM = dint("atm", (T, D), BF16)
    if 1 in layers:
        S1 = dint("s1", (T, D), BF16)
        XN1 = dint("xn1", (T, D), BF16)
        H1 = dint("h1", (T, D), BF16)
        ASP = dint("asp", (NT, D, 128), BF16)
        GD = dint("gd", (T, D), F32)

    def sbuf(name, shape, dtype):
        h = nc.sbuf_tensor(name, list(shape), dtype).__enter__()
        return h

    B0 = sbuf("B0", (128, 16640), BF16)
    B1 = sbuf("B1", (128, 16384), BF16)
    B2 = sbuf("B2", (128, 16384), BF16)
    B3 = sbuf("B3", (128, 16384), BF16)
    WS = sbuf("WS", (128, 8192), BF16)
    SCA = sbuf("SCA", (128, 1024), F32)
    SCB = sbuf("SCB", (128, 1024), F32)
    SCC = sbuf("SCC", (128, 2048), F32)
    ST = sbuf("ST", (128, 512), F32)
    M8 = sbuf("M8", (128, 128), F32)
    IDBF = sbuf("c_idbf", (128, 128), BF16)
    IDF32 = sbuf("c_idf32", (128, 128), F32)
    ONESROW = sbuf("c_onesrow", (1, 128), F32)
    ONESROWB = sbuf("c_onesrowb", (1, 128), BF16)
    ONESCOL = sbuf("c_onescol", (128, 1), BF16)
    PIDX = sbuf("c_pidx", (128, 1), F32)
    FB = sbuf("c_fbias", (128, 4), F32)
    SM = sbuf("SM", (128, 64), F32)
    LNW = sbuf("LNW", (128, 1024), F32)
    XTA = sbuf("XTA", (128, 1024), BF16)
    XTB = sbuf("XTB", (128, 1024), BF16)
    WOB = sbuf("WOB", (128, 1024), BF16)
    ASL = sbuf("ASL", (128, 1024), BF16)
    ADD = sbuf("ADD", (128, 1024), BF16)

    def load_consts():
        S.phase("consts")
        for t_, n_ in ((IDBF, "IDBF"), (IDF32, "IDF32"), (ONESROW, "ONESROW"),
                       (ONESROWB, "ONESROWB"), (ONESCOL, "ONESCOL"),
                       (PIDX, "PIDX")):
            S.dma("sync", t_[:, :], dram[n_], "d_ld")
        S.op("vector", lambda e: e.memset(FB[:, 0:1], 1e-5), inc="vec")
        S.op("vector", lambda e: e.memset(FB[:, 1:2], 1e-30), inc="vec")
        S.op("sync", lambda e: e.nop(), waits=S.dtok("d_ld"))

    def norm_phase(src_dram, fm_buf, tm_dram, tag, kind, w_dram=None, ntiles=NT):
        S.phase(tag)
        L = Locals()
        PTh = L.ps("pt_" + tag, [128, 1024], BF16)
        if kind == "rms":
            PWh = L.ps("pw_" + tag, [128, 1024], F32)
            ROWC = L.sb("rw_" + tag, [1, 1024], F32)
            ldw = S.dma("sync", ROWC[:, :], w_dram, "d_ld")
            last = None
            for seg in range(2):
                last = S.op("tensor", lambda e, seg=seg: e.matmul(
                    PWh[:, seg * 512:(seg + 1) * 512], ONESROW[:, :],
                    ROWC[:, seg * 512:(seg + 1) * 512],
                    start=True, stop=True), waits=[ldw],
                    inc="mm")
                S.op("vector", lambda e, seg=seg: e.tensor_copy(
                    LNW[:, seg * 512:(seg + 1) * 512],
                    PWh[:, seg * 512:(seg + 1) * 512]),
                    waits=[("mm", last)], inc="vec")
        st_after = {}
        for t in range(max(0, ntiles)):
            if kind == "ln":
                buf = (SCA, SCB)[t % 2]
            else:
                buf = (XTA, XTB)[t % 2]
            xnb = (XTA, XTB)[t % 2] if kind == "ln" else None
            ldw_ = ([("vec", S.c("vec")), ("mm", S.c("mm"))] if t >= 2 else [])
            if tm_dram is not None and t >= 2:
                ldw_.append(st_after[t - 2])
            ldv = S.dma("sync", buf[:, :], src_dram[t * 128:(t + 1) * 128, :],
                        "d_ld", waits=ldw_)
            if kind == "ln":
                S.op("vector", lambda e, b=buf, t=t: e.reduce_sum(
                    ST[:, t:t + 1], b[:, :], axis=AX.X), waits=[ldv])
                S.op("vector", lambda e, b=buf: e.tensor_tensor(
                    SCC[:, 0:1024], b[:, :], b[:, :], ALU.mult))
                S.op("vector", lambda e, t=t: e.reduce_sum(
                    ST[:, 16 + t:17 + t], SCC[:, 0:1024], axis=AX.X))
                pv = S.op("vector", lambda e, t=t: e.tensor_scalar(
                    ST[:, 32 + t:33 + t], ST[:, t:t + 1], 1.0 / D, None,
                    ALU.mult), waits=[("vec", S.c("vec"))], inc="vec")
                pv = S.op("vector", lambda e, t=t: e.tensor_tensor(
                    ST[:, 48 + t:49 + t], ST[:, 32 + t:33 + t],
                    ST[:, 32 + t:33 + t], ALU.mult),
                    waits=[("vec", pv)], inc="vec")
                pv = S.op("vector", lambda e, t=t: e.tensor_scalar(
                    ST[:, 64 + t:65 + t], ST[:, 16 + t:17 + t], 1.0 / D,
                    None, ALU.mult), waits=[("vec", pv)], inc="vec")
                vv = S.op("vector", lambda e, t=t: e.tensor_tensor(
                    ST[:, 80 + t:81 + t], ST[:, 64 + t:65 + t],
                    ST[:, 48 + t:49 + t], ALU.subtract),
                    waits=[("vec", pv)], inc="vec")
                S.op("scalar", lambda e, t=t: e.activation(
                    ST[:, 96 + t:97 + t], ST[:, 80 + t:81 + t], AF.Ln, bias=FB[:, 0:1]),
                    waits=[("vec", vv)])
                aa = S.op("scalar", lambda e, t=t: e.activation(
                    ST[:, 112 + t:113 + t], ST[:, 96 + t:97 + t], AF.Exp,
                    scale=-0.5), inc="act")
                vv2 = S.op("vector", lambda e, b=buf, xb=xnb, t=t: e.tensor_scalar(
                    xb[:, :], b[:, :], ST[:, 32 + t:33 + t],
                    ST[:, 112 + t:113 + t], ALU.subtract, ALU.mult),
                    waits=[("act", aa)], inc="vec")
                tsrc = xnb
            else:
                S.op("vector", lambda e, b=buf: e.tensor_tensor(
                    SCC[:, 0:1024], b[:, :], b[:, :], ALU.mult),
                    waits=[ldv])
                vv = S.op("vector", lambda e, t=t: e.reduce_sum(
                    ST[:, t:t + 1], SCC[:, 0:1024], axis=AX.X), inc="vec")
                S.op("scalar", lambda e, t=t: e.activation(
                    ST[:, 16 + t:17 + t], ST[:, t:t + 1], AF.Ln, bias=FB[:, 0:1],
                    scale=1.0 / D), waits=[("vec", vv)])
                aa = S.op("scalar", lambda e, t=t: e.activation(
                    ST[:, 32 + t:33 + t], ST[:, 16 + t:17 + t], AF.Exp,
                    scale=-0.5), inc="act")
                S.op("vector", lambda e, b=buf, t=t: e.tensor_scalar(
                    SCC[:, 0:1024], b[:, :], ST[:, 32 + t:33 + t], None,
                    ALU.mult), waits=[("act", aa)])
                vv2 = S.op("vector", lambda e, b=buf: e.tensor_tensor(
                    b[:, :], SCC[:, 0:1024], LNW[:, 0:1024], ALU.mult),
                    inc="vec")
                tsrc = buf
            if tm_dram is not None:
                st_after[t] = S.dma("scalar",
                                    tm_dram[t * 128:(t + 1) * 128, :],
                                    tsrc[:, :], "d_st", waits=[("vec", vv2)])
            last = None
            for c in range(NDC):
                last = S.op("tensor", lambda e, xb=tsrc, c=c: e.transpose(
                    PTh[:, c * 128:(c + 1) * 128], xb[:, c * 128:(c + 1) * 128],
                    IDBF[:, :]), waits=([("vec", vv2)] if c == 0 else []),
                    inc=("mm" if c == NDC - 1 else None))

            def evac(e, t=t):
                o = fm_buf[:, 0:16384].rearrange("p (c n) -> p c n", n=2048)
                o = o[:, :, t * 128:(t + 1) * 128]
                i = PTh[:, :].rearrange("p (c j) -> p c j", j=128)
                return e.tensor_copy(o, i)
            S.op("vector", evac, waits=[("mm", last)], inc="vec")
        if tm_dram is not None:
            S.op("scalar", lambda e: e.nop(), waits=S.dtok("d_st"))
        L.done()

    def vtm_phase(fm_buf, wT_dram, out_buf, aug65, tag, scale=None):
        S.phase(tag)
        with nc.psum_tensor("pva_" + tag, [128, 512], F32) as PA, \
             nc.psum_tensor("pvb_" + tag, [128, 512], F32) as PB:
            src = wT_dram.rearrange("(c p) n -> p c n", p=128)
            dst = WS[:, 0:8192].rearrange("p (c n) -> p c n", n=1024)
            ldv = S.dma("gpsimd", dst, src, "d_ld")
            if aug65:
                o = out_buf[:, 0:NT * 1040].rearrange("p (t h n) -> p t h n",
                                                      h=NH, n=65)
                S.op("vector", lambda e, o=o: e.memset(o[:, :, :, 64:65], 1.0),
                     inc="vec")
            evs = {}
            for t in range(NT):
                for seg in range(2):
                    P = (PA, PB)[(t * 2 + seg) % 2]
                    last = None
                    for i in range(NDC):
                        w_ = []
                        if t == 0 and seg == 0 and i == 0:
                            w_.append(ldv)
                        if i == 0 and (t * 2 + seg) >= 2:
                            w_.append(("vec", evs[(t * 2 + seg) - 2]))
                        last = S.op("tensor", lambda e, i=i, t=t, seg=seg, P=P:
                                    e.matmul(
                            P[:, :],
                            fm_buf[:, i * 2048 + t * 128:i * 2048 + (t + 1) * 128],
                            WS[:, i * 1024 + seg * 512:i * 1024 + (seg + 1) * 512],
                            start=(i == 0), stop=(i == NDC - 1)),
                            waits=w_, inc=("mm" if i == NDC - 1 else None))
                    if aug65:
                        def ev(e, t=t, seg=seg, P=P):
                            o = out_buf[:, t * 1040:(t + 1) * 1040]
                            o = o.rearrange("p (h n) -> p h n", n=65)
                            o = o[:, seg * 8:(seg + 1) * 8, 0:64]
                            i_ = P[:, :].rearrange("p (h n) -> p h n", n=64)
                            return e.tensor_copy(o, i_)
                        evs[t * 2 + seg] = S.op("vector", ev,
                                                waits=[("mm", last)], inc="vec")
                    elif scale is None:
                        evs[t * 2 + seg] = S.op(
                            "vector", lambda e, t=t, seg=seg, P=P: e.tensor_copy(
                                out_buf[:, t * 1024 + seg * 512:
                                        t * 1024 + (seg + 1) * 512], P[:, :]),
                            waits=[("mm", last)], inc="vec")
                    else:
                        evs[t * 2 + seg] = S.op(
                            "vector", lambda e, t=t, seg=seg, P=P: e.tensor_scalar(
                                out_buf[:, t * 1024 + seg * 512:
                                        t * 1024 + (seg + 1) * 512],
                                P[:, :], scale, None, ALU.mult),
                            waits=[("mm", last)], inc="vec")

    def qk_chunk(fm_buf, wT_dram, c, out_sb, PA, PB, scale=None, wslot=0):
        src = wT_dram[:, c * 128:(c + 1) * 128].rearrange("(i p) n -> p i n",
                                                          p=128)
        dst = WS[:, wslot:wslot + 1024].rearrange("p (i n) -> p i n", n=128)
        ldv = S.dma("gpsimd", dst, src, "d_ld",
                    waits=[("mm", S.c("mm"))])
        evs = {}
        for seg in range(4):
            P = (PA, PB)[seg % 2]
            last = None
            for i in range(NDC):
                w_ = []
                if seg == 0 and i == 0:
                    w_.append(ldv)
                if i == 0 and seg >= 2:
                    w_.append(("vec", evs[seg - 2]))
                last = S.op("tensor", lambda e, i=i, seg=seg, P=P: e.matmul(
                    P[:, :], WS[:, wslot + i * 128:wslot + (i + 1) * 128],
                    fm_buf[:, i * 2048 + seg * 512:i * 2048 + (seg + 1) * 512],
                    start=(i == 0), stop=(i == NDC - 1)),
                    waits=w_, inc=("mm" if i == NDC - 1 else None))
            if scale is None:
                evs[seg] = S.op("vector", lambda e, seg=seg, P=P: e.tensor_copy(
                    out_sb[:, seg * 512:(seg + 1) * 512], P[:, :]),
                    waits=[("mm", last)], inc="vec")
            else:
                evs[seg] = S.op("vector", lambda e, seg=seg, P=P: e.tensor_scalar(
                    out_sb[:, seg * 512:(seg + 1) * 512], P[:, :], scale, None,
                    ALU.mult), waits=[("mm", last)], inc="vec")

    def wo_proj(src_mode, wT_dram, add_dram, out_dram, tag):
        S.phase(tag)
        with nc.psum_tensor("pwt_" + tag, [128, 1024], BF16) as PT, \
             nc.psum_tensor("pwo_" + tag, [128, 1024], F32) as PO:
            src = wT_dram.rearrange("(c p) n -> p c n", p=128)
            dst = WS[:, 0:8192].rearrange("p (c n) -> p c n", n=1024)
            ldv = S.dma("gpsimd", dst, src, "d_ld")
            st_prev = ("d_st@sync", 0)
            for t in range(NT):
                if add_dram is not None:
                    lda = S.dma("sync", ADD[:, :],
                                add_dram[t * 128:(t + 1) * 128, :], "d_ld",
                                waits=([("vec", S.c("vec"))] if t >= 1 else ()))
                if src_mode[0] == "dram_tm":
                    ldt = S.dma("scalar", XTA[:, :],
                                src_mode[1][t * 128:(t + 1) * 128, :], "d_ld",
                                waits=([("mm", S.c("mm"))] if t >= 1 else ()))
                    last = None
                    for c in range(NDC):
                        last = S.op("tensor", lambda e, c=c: e.transpose(
                            PT[:, c * 128:(c + 1) * 128],
                            XTA[:, c * 128:(c + 1) * 128], IDBF[:, :]),
                            waits=([ldt, ("vec", S.c("vec"))]
                                   if c == 0 else []),
                            inc=("mm" if c == NDC - 1 else None))
                    avv = S.op("vector", lambda e: e.tensor_copy(
                        ASL[:, :], PT[:, :]), waits=[("mm", last)], inc="vec")
                    asl_ready = [("vec", avv)]
                else:
                    lda2 = S.dma("scalar",
                                 ASL[:, :].rearrange("p (c n) -> p c n", n=128),
                                 dram["asp"][t].rearrange("(c p) n -> p c n",
                                                          p=128),
                                 "d_ld",
                                 waits=([("mm", S.c("mm"))] if t >= 1 else ()))
                    asl_ready = [lda2]
                last = None
                for seg in range(2):
                    for c in range(NDC):
                        w_ = list(asl_ready) if (c == 0 and seg == 0) else []
                        if t == 0 and seg == 0 and c == 0:
                            w_.append(ldv)
                        if seg == 0 and c == 0 and t >= 1:
                            w_.append(("vec", S.c("vec")))
                        last = S.op("tensor", lambda e, c=c, seg=seg: e.matmul(
                            PO[:, seg * 512:(seg + 1) * 512],
                            ASL[:, c * 128:(c + 1) * 128],
                            WS[:, c * 1024 + seg * 512:c * 1024 + (seg + 1) * 512],
                            start=(c == 0), stop=(c == NDC - 1)),
                            waits=w_,
                            inc=("mm" if (seg == 1 and c == NDC - 1) else None))
                vv = None
                for seg in range(2):
                    w_ = [("mm", last), st_prev]
                    if add_dram is not None:
                        vv = S.op("vector", lambda e, seg=seg: e.tensor_tensor(
                            WOB[:, seg * 512:(seg + 1) * 512],
                            PO[:, seg * 512:(seg + 1) * 512],
                            ADD[:, seg * 512:(seg + 1) * 512], ALU.add),
                            waits=w_ + [lda], inc="vec")
                    else:
                        vv = S.op("vector", lambda e, seg=seg: e.tensor_copy(
                            WOB[:, seg * 512:(seg + 1) * 512],
                            PO[:, seg * 512:(seg + 1) * 512]),
                            waits=w_, inc="vec")
                st_prev = S.dma("sync", out_dram[t * 128:(t + 1) * 128, :],
                                WOB[:, :], "d_st", waits=[("vec", vv)])
            S.op("sync", lambda e: e.nop(), waits=S.dtok("d_st"))

    def l0_attn():
        vtm_phase(B2, dram["L0WvT"], B0, False, "l0v")
        S.phase("l0attn")
        QF = B3
        KF = B1
        QC = B3[:, 8192:10240]
        KC = B3[:, 10240:12288]
        KVS = B3[:, 12288:12416]
        KSS = B3[:, 12416:12418]
        HSTRIP = B3[:, 13312:14336]
        with nc.psum_tensor("pl0a", [128, 512], F32) as PA, \
             nc.psum_tensor("pl0b", [128, 512], F32) as PB, \
             nc.psum_tensor("pl0f", [128, 512], F32) as PF, \
             nc.psum_tensor("pl0kv", [128, 512], F32) as PKV, \
             nc.psum_tensor("pl0ks", [128, 512], F32) as PKS, \
             nc.psum_tensor("pl0o", [128, 512], F32) as PO, \
             nc.psum_tensor("pl0n", [128, 512], F32) as PN:
            ldwf1 = S.dma("gpsimd", WS[0:64, 4096:4352], dram["L0WfT"], "d_ld")
            ldwf2 = S.dma("gpsimd", WS[64:128, 4352:4608], dram["L0WfT"],
                          "d_ld")
            for c in range(NDC):
                if L0A_MASK is None or 1 in L0A_MASK:
                    qk_chunk(B2, dram["L0WqT"], c, QC, PA, PB, wslot=0)
                    qk_chunk(B2, dram["L0WkT"], c, KC, PA, PB, wslot=1024)
                for hh in range(2):
                    if L0A_STEPS < 2 and not (L0A_MASK and 9 in L0A_MASK):
                        continue
                    H_ = 2 * c + hh
                    r0 = 64 * hh
                    stt_tok = {}
                    exp_tok = {}
                    banks = (PF, PA)
                    for kk in range(24):
                        p = kk % 2
                        P_ = banks[p]
                        smin = SCC[:, 1024 * p:1024 * p + 512]
                        sexp = SCC[:, 1024 * p + 512:1024 * p + 1024]
                        w_ = []
                        if kk - 2 in stt_tok:
                            w_.append(("vec", stt_tok[kk - 2]))
                        else:
                            w_.append(("vec", S.c("vec")))
                        if kk == 0:
                            w_ += [ldwf1, ldwf2, ("act", S.c("act"))]
                        if kk < 8:
                            fc, seg = kk // 4, kk % 4
                            width = 512
                            mmv = S.op("tensor",
                                       lambda e, fc=fc, seg=seg, r0=r0, P_=P_:
                                       e.matmul(
                                           P_[:, 0:512],
                                           WS[r0:r0 + 64,
                                              4096 + r0 * 4 + fc * 128:
                                              4096 + r0 * 4 + (fc + 1) * 128],
                                           QC[r0:r0 + 64,
                                              seg * 512:(seg + 1) * 512],
                                           start=True, stop=True),
                                       waits=w_, inc="mm")
                        else:
                            t = kk - 8
                            width = 256
                            mmv = S.op("tensor",
                                       lambda e, t=t, r0=r0, P_=P_: e.matmul(
                                           P_[:, 0:256],
                                           KC[r0:r0 + 64,
                                              t * 128:(t + 1) * 128],
                                           WS[r0:r0 + 64,
                                              4096 + r0 * 4:4352 + r0 * 4],
                                           start=True, stop=True),
                                       waits=w_, inc="mm")
                        vw = [("mm", mmv)]
                        if kk - 2 in exp_tok:
                            vw.append(("act", exp_tok[kk - 2]))
                        vv = S.op("vector", lambda e, P_=P_, smin=smin,
                                  width=width: e.tensor_scalar(
                                      smin[:, 0:width], P_[:, 0:width], 0.0,
                                      None, ALU.min),
                                  waits=vw, inc="vec")
                        exp_tok[kk] = S.op("scalar", lambda e, smin=smin,
                                           sexp=sexp, width=width:
                                           e.activation(sexp[:, 0:width],
                                                        smin[:, 0:width],
                                                        AF.Exp),
                                           waits=[("vec", vv)], inc="act")
                        if kk < 8:
                            dst = QF[:, fc * 2048 + seg * 512:
                                     fc * 2048 + (seg + 1) * 512]
                        else:
                            dst = KF[:, t * 256:(t + 1) * 256]
                        stt_tok[kk] = S.op("vector", lambda e, dst=dst, P_=P_,
                                           sexp=sexp, width=width:
                                           e.scalar_tensor_tensor(
                                               dst, P_[:, 0:width], 0.0,
                                               sexp[:, 0:width],
                                               ALU.max, ALU.add),
                                           waits=[("act", exp_tok[kk])],
                                           inc="vec")
                    kfv = S.c("vec")
                    for fc in range(2 if (L0A_STEPS >= 4 if L0A_MASK is None else 4 in L0A_MASK) else 0):
                        for t in range(NT):
                            S.op("tensor", lambda e, fc=fc, t=t, H_=H_: e.matmul(
                                PKV[:, fc * 64:(fc + 1) * 64],
                                KF[:, t * 256 + fc * 128:t * 256 + (fc + 1) * 128],
                                B0[:, t * 1024 + H_ * 64:t * 1024 + (H_ + 1) * 64],
                                start=(t == 0), stop=(t == NT - 1)),
                                waits=([("vec", kfv)]
                                       if (t == 0 and fc == 0) else []),
                                inc=("mm" if (fc == 1 and t == NT - 1) else None))
                            S.op("tensor", lambda e, fc=fc, t=t: e.matmul(
                                PKS[:, fc:fc + 1],
                                KF[:, t * 256 + fc * 128:t * 256 + (fc + 1) * 128],
                                ONESCOL[:, :], start=(t == 0),
                                stop=(t == NT - 1)),
                                inc=("mm" if (fc == 1 and t == NT - 1) else None))
                    mmv = S.c("mm")
                    S.op("vector", lambda e: e.tensor_copy(
                        KVS[:, :].rearrange("p (f n) -> p f n", n=64),
                        PKV[:, 0:128].rearrange("p (f n) -> p f n", n=64)),
                        waits=[("mm", mmv)])
                    vv = S.op("vector", lambda e: e.tensor_copy(
                        KSS[:, :], PKS[:, 0:2]), inc="vec")
                    if DBG_BAR >= 1:
                        S.phase("l0attn")
                    ev_prev = 0
                    for t in range(NT):
                        b8 = t % 8
                        for fc in range(2):
                            w_ = []
                            if fc == 0 and t % 8 == 0:
                                w_ = [("vec", ev_prev if t else 0)]
                                if t == 0:
                                    w_ = [("vec", vv)]
                            S.op("tensor", lambda e, t=t, fc=fc, b8=b8: e.matmul(
                                PO[:, b8 * 64:(b8 + 1) * 64],
                                QF[:, fc * 2048 + t * 128:fc * 2048 + (t + 1) * 128],
                                KVS[:, fc * 64:(fc + 1) * 64],
                                start=(fc == 0), stop=(fc == 1)),
                                waits=w_, inc=("mm" if fc == 1 else None))
                            S.op("tensor", lambda e, t=t, fc=fc, b8=b8: e.matmul(
                                PN[:, b8:b8 + 1],
                                QF[:, fc * 2048 + t * 128:fc * 2048 + (t + 1) * 128],
                                KSS[:, fc:fc + 1], start=(fc == 0),
                                stop=(fc == 1)),
                                inc=("mm" if fc == 1 else None))
                        if b8 == 7:
                            mmv = S.c("mm")
                            sv = S.op("vector", lambda e: e.tensor_scalar(
                                SM[:, 0:8], PN[:, 0:8], 1e-6, None, ALU.add),
                                waits=[("mm", mmv)], inc="vec")
                            rv = S.op("vector", lambda e: e.reciprocal(
                                SM[:, 8:16], SM[:, 0:8]),
                                waits=[("vec", sv)], inc="vec")
                            for j8 in range(8):
                                tt = t - 7 + j8
                                ev_prev = S.op(
                                    "vector", lambda e, tt=tt, j8=j8:
                                    e.tensor_scalar(
                                        HSTRIP[:, tt * 64:(tt + 1) * 64],
                                        PO[:, j8 * 64:(j8 + 1) * 64],
                                        SM[:, 8 + j8:9 + j8], None, ALU.mult),
                                    waits=[("vec", rv)], inc="vec")
                    if (L0A_STEPS < 6 if L0A_MASK is None else 6 not in L0A_MASK):
                        continue
                    dstrip = dram["atm"][:, H_ * 64:(H_ + 1) * 64]
                    dstrip = dstrip.rearrange("(t p) n -> p t n", p=128)
                    sstrip = HSTRIP[:, :].rearrange("p (t n) -> p t n", n=64)
                    S.dma("sync", dstrip, sstrip, "d_st",
                          waits=[("vec", S.c("vec"))])
                    S.op("vector", lambda e: e.engine_nop(),
                         waits=S.dtok("d_st"), inc="vec")
        wo_proj(("dram_tm", dram["atm"]), dram["L0WoT"], None, S0, "l0wo")

    def l0_router():
        S.phase("l0rt")
        L = Locals()
        PL = L.ps("plog", [128, 512], F32)
        PM = L.ps("pm8f", [8, 128], F32)
        GW = L.sb("GWsb", [128, 64], BF16)
        LGT = L.sb("LGT", [128, 8], F32)
        EQ = L.sb("EQsb", [128, 8], F32)
        NS = L.sb("NSsb", [128, 4], F32)
        M8F = WS[0:8, 0:2048]
        src = dram["L0gw"].rearrange("(c p) n -> p c n", p=128)
        ldv = S.dma("gpsimd", GW[:, :].rearrange("p (c n) -> p c n", n=8),
                    src, "d_ld")
        for t in range(NT):
            last = None
            for i in range(NDC):
                last = S.op("tensor", lambda e, t=t, i=i: e.matmul(
                    PL[:, 0:8], B2[:, i * 2048 + t * 128:i * 2048 + (t + 1) * 128],
                    GW[:, i * 8:(i + 1) * 8], start=(i == 0),
                    stop=(i == NDC - 1)),
                    waits=([ldv] if (t == 0 and i == 0) else []) +
                          ([("vec", S.c("vec"))] if i == 0 else []),
                    inc=("mm" if i == NDC - 1 else None))
            lv = S.op("vector", lambda e: e.tensor_copy(LGT[:, :], PL[:, 0:8]),
                      waits=[("mm", last)], inc="vec")
            vv = S.op("vector", lambda e: e.reduce_max(
                NS[:, 0:1], LGT[:, :], axis=AX.X, negate=True),
                waits=[("vec", lv)], inc="vec")
            aa = S.op("scalar", lambda e: e.activation(
                SCC[:, 0:8], LGT[:, :], AF.Exp, bias=NS[:, 0:1],
                accum_out=NS[:, 1:2]), waits=[("vec", vv)], inc="act")
            rv = S.op("vector", lambda e: e.reciprocal(NS[:, 2:3], NS[:, 1:2]),
                      waits=[("act", aa)], inc="vec")
            pv = S.op("vector", lambda e: e.tensor_scalar(
                EQ[:, :], LGT[:, :], NS[:, 0:1], 0.0, ALU.add, ALU.is_ge),
                waits=[("vec", rv)], inc="vec")
            pv = S.op("vector", lambda e: e.memset(NS[:, 3:4], 1.0),
                      waits=[("vec", pv)], inc="vec")
            for ei in range(NE):
                pv = S.op("vector", lambda e, ei=ei: e.tensor_tensor(
                    SCC[:, 8:9], EQ[:, ei:ei + 1], NS[:, 3:4], ALU.mult),
                    waits=[("vec", pv)], inc="vec")
                pv = S.op("vector", lambda e, ei=ei, t=t: e.tensor_tensor(
                    M8[:, t * 8 + ei:t * 8 + ei + 1], SCC[:, 8:9], NS[:, 2:3],
                    ALU.mult), waits=[("vec", pv)], inc="vec")
                pv = S.op("vector", lambda e: e.tensor_tensor(
                    NS[:, 3:4], NS[:, 3:4], SCC[:, 8:9], ALU.subtract),
                    waits=[("vec", pv)], inc="vec")
            mt = S.op("tensor", lambda e, t=t: e.transpose(
                PM[:, :], M8[:, t * 8:(t + 1) * 8], IDF32[:, :]),
                waits=[("vec", S.c("vec"))], inc="mm")
            S.op("vector", lambda e, t=t: e.tensor_copy(
                M8F[:, t * 128:(t + 1) * 128], PM[:, :]),
                waits=[("mm", mt)], inc="vec")
        L.done()

    def l0_ffn():
        S.phase("l0ffn")
        L = Locals()
        PHa = L.ps("ph0a", [128, 512], F32)
        PHb = L.ps("ph0b", [128, 512], F32)
        PYa = L.ps("py0a", [128, 1024], F32)
        PYb = L.ps("py0b", [128, 1024], F32)
        PB1 = L.ps("pb1", [128, 128], F32)
        B2ALL = L.sb("B2ALL", [8, 1024], BF16)
        B1RAW = L.sb("B1RAW", [32, 128], F32)
        B1C = L.sb("B1C", [128, 64], F32)
        M8F = WS[0:8, 0:2048]
        ldb2 = S.dma("gpsimd", B2ALL[:, :], dram["L0b2"], "d_ld")
        for ts in range(NT):
            P = (PYa, PYb)[ts % 2]
            last = None
            for seg in range(2):
                last = S.op("tensor", lambda e, ts=ts, seg=seg, P=P: e.matmul(
                    P[:, seg * 512:(seg + 1) * 512],
                    M8F[:, ts * 128:(ts + 1) * 128],
                    B2ALL[:, seg * 512:(seg + 1) * 512],
                    start=True, stop=True),
                    waits=([ldb2] if (ts == 0 and seg == 0) else []) +
                          ([("vec", S.c("vec"))] if (seg == 0 and ts >= 2)
                           else []),
                    inc=("mm" if seg == 1 else None))
            S.op("vector", lambda e, ts=ts, P=P: e.tensor_copy(
                B3[:, ts * 1024:(ts + 1) * 1024], P[:, :]),
                waits=[("mm", last)], inc="vec")
        w1b = [B0[:, 0:8192], B0[:, 8192:16384]]
        w2b = [B1[:, 0:8192], B1[:, 8192:16384]]
        hb = [WS[:, 0:4096], WS[:, 4096:8192]]
        ld = {}
        mm_after_h = {}
        mm_after_y = {}

        def issue_loads(e_, q_, slot):
            key_prev = (e_ * 4 + q_) - 2
            w1w = []
            w2w = []
            if key_prev >= 0:
                w1w = [("mm", mm_after_h[key_prev])]
                w2w = [("mm", mm_after_y[(key_prev, 3)])]
            s1 = dram["L0w1"][e_][:, q_ * 1024:(q_ + 1) * 1024]
            s1 = s1.rearrange("(i p) n -> p i n", p=128)
            d1 = w1b[slot].rearrange("p (i n) -> p i n", n=1024)
            v1 = S.dma("gpsimd", d1, s1, "d_w1", waits=w1w)
            s2 = dram["L0w2"][e_][q_ * 1024:(q_ + 1) * 1024, :]
            s2 = s2.rearrange("(f p) n -> p f n", p=128)
            d2 = w2b[slot].rearrange("p (f n) -> p f n", n=1024)
            v2 = S.dma("gpsimd", d2, s2, "d_w2", waits=w2w)
            ld[(e_, q_)] = (v1, v2)

        issue_loads(0, 0, 0)
        issue_loads(0, 1, 1)
        for e_ in range(NE):
            lb1 = S.dma("sync", B1RAW[:, :], dram["L0b1"][e_], "d_ld",
                        waits=([("act", S.c("act"))] if e_ > 0 else ()))
            bt = S.op("tensor", lambda e: e.transpose(
                PB1[:, 0:32], B1RAW[:, :], IDF32[0:32, 0:32]),
                waits=[lb1] +
                      ([("vec", S.c("vec"))] if e_ > 0 else []), inc="mm")
            S.op("vector", lambda e: e.tensor_copy(B1C[:, 0:32], PB1[:, 0:32]),
                 waits=[("mm", bt)] +
                       ([("act", S.c("act"))] if e_ > 0 else []), inc="vec")
            b1v = S.c("vec")
            for q_ in range(4):
                key = e_ * 4 + q_
                slot = key % 2
                v1, v2 = ld[(e_, q_)]
                for tc in range(4):
                    hslot = tc % 2
                    hw = []
                    if key >= 1 or tc >= 2:
                        pk, ptc = (key, tc - 2) if tc >= 2 else (key - 1, tc + 2)
                        if (pk, ptc) in mm_after_y:
                            hw = [("mm", mm_after_y[(pk, ptc)])]
                    for fs in range(8):
                        PH = (PHa, PHb)[fs % 2]
                        last = None
                        for i in range(NDC):
                            w_ = []
                            if i == 0:
                                w_ += [("act", S.c("act"))]
                                if fs == 0:
                                    w_ += [v1] + hw
                            last = S.op("tensor",
                                        lambda e, fs=fs, tc=tc, i=i, slot=slot,
                                        PH=PH: e.matmul(
                                PH[:, :],
                                w1b[slot][:, i * 1024 + fs * 128:
                                          i * 1024 + (fs + 1) * 128],
                                B2[:, i * 2048 + tc * 512:i * 2048 + (tc + 1) * 512],
                                start=(i == 0), stop=(i == NDC - 1)),
                                waits=w_, inc=("mm" if i == NDC - 1 else None))
                            if fs == 7 and i == NDC - 1:
                                mm_after_h[key] = S.c("mm")
                        S.op("scalar", lambda e, fs=fs, q_=q_, hslot=hslot,
                             PH=PH, last=last: e.activation(
                                 hb[hslot][:, fs * 512:(fs + 1) * 512], PH[:, :],
                                 AF.Gelu, bias=B1C[:, q_ * 8 + fs:q_ * 8 + fs + 1]),
                             waits=[("mm", last), ("vec", b1v)], inc="act")
                    hact = S.c("act")
                    for tsl in range(4):
                        ts = tc * 4 + tsl
                        PY = (PYa, PYb)[ts % 2]
                        last = None
                        for fs in range(8):
                            for seg in range(2):
                                w_ = []
                                if fs == 0 and seg == 0:
                                    w_ = [("act", hact), v2,
                                          ("vec", S.c("vec"))]
                                last = S.op("tensor",
                                            lambda e, fs=fs, seg=seg, tsl=tsl,
                                            slot=slot, hslot=hslot, PY=PY:
                                            e.matmul(
                                    PY[:, seg * 512:(seg + 1) * 512],
                                    hb[hslot][:, fs * 512 + tsl * 128:
                                              fs * 512 + (tsl + 1) * 128],
                                    w2b[slot][:, fs * 1024 + seg * 512:
                                              fs * 1024 + (seg + 1) * 512],
                                    start=(fs == 0), stop=(fs == 7)),
                                    waits=w_,
                                    inc=("mm" if (fs == 7 and seg == 1)
                                         else None))
                        mm_after_y[(key, tc)] = last
                        S.op("vector", lambda e, ts=ts, e_=e_, PY=PY, last=last:
                             e.scalar_tensor_tensor(
                                 B3[:, ts * 1024:(ts + 1) * 1024], PY[:, :],
                                 M8[:, ts * 8 + e_:ts * 8 + e_ + 1],
                                 B3[:, ts * 1024:(ts + 1) * 1024],
                                 ALU.mult, ALU.add),
                             waits=[("mm", last)], inc="vec")
                nkey = key + 2
                if nkey < NE * 4:
                    issue_loads(nkey // 4, nkey % 4, nkey % 2)
        L.done()

    def l0_combine():
        S.phase("l0cmb")
        st_prev = ("d_st@sync", 0)
        for t in range(NT):
            sstage = (XTA, XTB)[t % 2]
            lds = S.dma("sync", sstage[:, :], S0[t * 128:(t + 1) * 128, :],
                        "d_ld", waits=([("vec", S.c("vec"))] if t >= 2 else ()))
            xstage = (SCA, SCB)[t % 2]
            ldx = S.dma("scalar", xstage[:, :],
                        dram["x"][t * 128:(t + 1) * 128, :], "d_ld")
            aa = S.op("scalar", lambda e, t=t: e.activation(
                WOB[:, :], B3[:, t * 1024:(t + 1) * 1024], AF.Silu),
                waits=[("vec", S.c("vec")), ("act", S.c("act"))], inc="act")
            vv = S.op("vector", lambda e, s=sstage: e.tensor_tensor(
                SCC[:, 0:1024], s[:, :], WOB[:, :], ALU.mult),
                waits=[("act", aa), lds], inc="vec")
            vv = S.op("vector", lambda e, xs=xstage: e.tensor_tensor(
                SCC[:, 1024:2048], xs[:, :], SCC[:, 0:1024], ALU.add),
                waits=[ldx, st_prev], inc="vec")
            st_prev = S.dma("sync", XRES[t * 128:(t + 1) * 128, :],
                            SCC[:, 1024:2048], "d_st", waits=[("vec", vv)])
        S.op("sync", lambda e: e.nop(), waits=S.dtok("d_st"))

    def l1_retention():
        vtm_phase(B2, dram["L1WvT"], B0, False, "l1v")
        S.phase("l1qk")
        with nc.psum_tensor("pl1qa", [128, 512], F32) as PA, \
             nc.psum_tensor("pl1qb", [128, 512], F32) as PB:
            for c in range(NDC):
                qk_chunk(B2, dram["L1WqT"], c, B3[:, c * 2048:(c + 1) * 2048],
                         PA, PB, scale=0.125, wslot=0)
                qk_chunk(B2, dram["L1WkT"], c, B1[:, c * 2048:(c + 1) * 2048],
                         PA, PB, wslot=1024)
        S.phase("l1ret")
        L = Locals()
        PSA = L.ps("pretS", [128, 2048], F32)
        POUT = L.ps("pretO", [128, 2048], F32)
        UROWs = L.sb("urow_l", [128, T], BF16)
        CTRIs = L.sb("ctri_l", [128, 128], BF16)
        CROW = L.sb("crow_l", [128, T], BF16)
        LNG = L.sb("lng_l", [128, 16], F32)
        GROW = L.sb("grow_l", [1, 32], F32)
        WTMP = L.sb("wtmp_l", [128, 2048], BF16)
        WSB = L.sb("wsb_l", [128, 2048], BF16)
        S.dma("sync", UROWs[:, :], dram["UROW"], "d_ld")
        S.dma("sync", CTRIs[:, :], dram["CTRI"], "d_ld")
        ldg = S.dma("sync", GROW[:, 0:16], dram["gammas"], "d_ld")
        aa = S.op("scalar", lambda e: e.activation(
            GROW[:, 16:32], GROW[:, 0:16], AF.Ln),
            waits=S.dtok("d_ld"), inc="act")
        lm = S.op("tensor", lambda e: e.matmul(
            PSA[:, 0:16], ONESROW[:, :], GROW[:, 16:32], start=True, stop=True),
            waits=[("act", aa)], inc="mm")
        S.op("vector", lambda e: e.tensor_copy(LNG[:, :], PSA[:, 0:16]),
             waits=[("mm", lm)], inc="vec")
        for H_ in range(NH):
            c, hh = H_ // 2, H_ % 2
            r0 = 64 * hh
            Q = B3[:, c * 2048:(c + 1) * 2048]
            K = B1[:, c * 2048:(c + 1) * 2048]
            pv = S.op("vector", lambda e, H_=H_: e.tensor_scalar(
                SM[:, 0:1], PIDX[:, :], LNG[:, H_:H_ + 1], None, ALU.mult),
                waits=[("vec", S.c("vec"))], inc="vec")
            aa = S.op("scalar", lambda e: e.activation(
                SM[:, 1:2], SM[:, 0:1], AF.Exp, scale=-1.0),
                waits=[("vec", pv)], inc="act")
            S.op("vector", lambda e, H_=H_: e.tensor_scalar(
                SCC[:, 0:2048], UROWs[:, :], LNG[:, H_:H_ + 1], None, ALU.mult),
                inc="vec")
            ca = S.op("scalar", lambda e: e.activation(
                CROW[:, :], SCC[:, 0:2048], AF.Exp),
                waits=[("vec", S.c("vec"))], inc="act")
            for j in range(NT):
                s0 = j * 128
                width = T - s0
                last = None
                for (alo, ahi) in pieces(0, width):
                    last = S.op("tensor", lambda e, j=j, alo=alo, ahi=ahi,
                                r0=r0, s0=s0, Q=Q, K=K: e.matmul(
                        PSA[:, alo:ahi], K[r0:r0 + 64, s0:s0 + 128],
                        Q[r0:r0 + 64, s0 + alo:s0 + ahi],
                        start=True, stop=True),
                        waits=([("vec", S.c("vec")), ("act", ca)]
                               if alo == 0 else []),
                        inc=("mm" if ahi == width else None))
                vv = S.op("vector", lambda e, width=width: e.tensor_scalar(
                    WTMP[:, 0:width], PSA[:, 0:width], SM[:, 1:2], None,
                    ALU.mult), waits=[("mm", last), ("act", aa)], inc="vec")
                vv = S.op("vector", lambda e, width=width: e.tensor_tensor(
                    WSB[:, 0:width], WTMP[:, 0:width], CROW[:, 0:width],
                    ALU.mult), inc="vec")
                vv = S.op("vector", lambda e: e.tensor_tensor(
                    WSB[:, 0:128], WSB[:, 0:128], CTRIs[:, :], ALU.mult),
                    inc="vec")
                for (alo, ahi) in pieces(s0, T):
                    bank = alo // 512
                    lastj = min(NT - 1, 4 * bank + 3)
                    S.op("tensor", lambda e, j=j, alo=alo, ahi=ahi, s0=s0,
                         H_=H_: e.matmul(
                        POUT[0:64, alo:ahi],
                        B0[:, j * 1024 + H_ * 64:j * 1024 + (H_ + 1) * 64],
                        WSB[:, alo - s0:ahi - s0],
                        start=(j == 0), stop=(j == lastj)),
                        waits=[("vec", vv)] if alo == s0 else [],
                        inc=("mm" if (j == NT - 1 and ahi == T) else None))
            mmv = S.c("mm")
            vv = S.op("vector", lambda e: e.tensor_copy(
                WTMP[0:64, :], POUT[0:64, :]), waits=[("mm", mmv)], inc="vec")
            dstrip = dram["asp"].rearrange("t r c -> r t c")[64 * hh + 128 * c:
                                                            64 * hh + 128 * c + 64]
            sstrip = WTMP[0:64, :].rearrange("p (t c) -> p t c", c=128)
            S.dma("sync", dstrip, sstrip, "d_st", waits=[("vec", vv)])
            S.op("vector", lambda e: e.engine_nop(),
                 waits=S.dtok("d_st"), inc="vec")
        L.done()
        wo_proj(("asp",), dram["L1WoT"], None, S1, "l1wo")

    def l1_softmax_attn():
        norm_phase(XN1, B2, None, "l1rms1", "rms", dram["ln1w"],
                   ntiles=rms_ntiles)
        vtm_phase(B2, dram["F1WvT"], B0, True, "f1v")
        S.phase("f1qk")
        with nc.psum_tensor("pf1qa", [128, 512], F32) as PA, \
             nc.psum_tensor("pf1qb", [128, 512], F32) as PB:
            for c in range(NDC):
                qk_chunk(B2, dram["F1WqT"], c, B3[:, c * 2048:(c + 1) * 2048],
                         PA, PB, scale=0.125, wslot=0)
                qk_chunk(B2, dram["F1WkT"], c, B1[:, c * 2048:(c + 1) * 2048],
                         PA, PB, wslot=1024)
        S.phase("f1sm")
        L = Locals()
        PSA = L.ps("psmS", [128, 2048], F32)
        POUT = L.ps("psmO", [128, 2048], F32)
        ROWS = L.sb("rows_l", [1, 4096], BF16)
        RNB = L.sb("rn_l", [1, 2048], BF16)
        RRF = L.sb("rr_l", [1, 2048], F32)
        NEGBS = L.sb("negbs_l", [128, 2048], BF16)
        PTS = L.sb("pts_l", [128, 2048], BF16)
        WTMP = PTS
        RECBS = NEGBS[0:64, :]
        Rq = ROWS[:, 0:2048]
        Rk = ROWS[:, 2048:4096]
        Rn = RNB[:, 0:2048]
        Rr = RRF[:, 0:2048]
        for H_ in range(NH):
            c, hh = H_ // 2, H_ % 2
            r0 = 64 * hh
            Q = B3[:, c * 2048:(c + 1) * 2048]
            K = B1[:, c * 2048:(c + 1) * 2048]
            vv = S.op("vector", lambda e, Q=Q, r0=r0: e.tensor_tensor(
                WTMP[0:64, :], Q[r0:r0 + 64, :], Q[r0:r0 + 64, :], ALU.mult),
                waits=[("mm", S.c("mm")), ("act", S.c("act"))], inc="vec")
            last = None
            for (alo, ahi) in pieces(0, T):
                last = S.op("tensor", lambda e, alo=alo, ahi=ahi: e.matmul(
                    PSA[0:1, alo:ahi], ONESCOL[0:64, :], WTMP[0:64, alo:ahi],
                    start=True, stop=True), waits=[("vec", vv)],
                    inc=("mm" if ahi == T else None))
            S.op("scalar", lambda e: e.activation(
                Rq, PSA[0:1, 0:2048], AF.Ln, bias=FB[0:1, 1:2]),
                waits=[("mm", last)])
            aq = S.op("scalar", lambda e: e.activation(
                Rq, Rq, AF.Exp, scale=0.5), inc="act")
            vv = S.op("vector", lambda e, K=K, r0=r0: e.tensor_tensor(
                WTMP[0:64, :], K[r0:r0 + 64, :], K[r0:r0 + 64, :], ALU.mult),
                waits=[("act", aq)], inc="vec")
            last = None
            for (alo, ahi) in pieces(0, T):
                last = S.op("tensor", lambda e, alo=alo, ahi=ahi: e.matmul(
                    PSA[0:1, alo:ahi], ONESCOL[0:64, :], WTMP[0:64, alo:ahi],
                    start=True, stop=True), waits=[("vec", vv)],
                    inc=("mm" if ahi == T else None))
            S.op("scalar", lambda e: e.activation(
                Rk, PSA[0:1, 0:2048], AF.Ln, bias=FB[0:1, 1:2]),
                waits=[("mm", last)])
            ak = S.op("scalar", lambda e: e.activation(
                Rk, Rk, AF.Exp, scale=0.5), inc="act")
            vv = S.op("vector", lambda e: e.reduce_max(
                SM[0:1, 0:1], Rk, axis=AX.X), waits=[("act", ak)], inc="vec")
            vv = S.op("vector", lambda e: e.tensor_scalar(
                Rn, Rq, SM[0:1, 0:1], -1.0, ALU.mult, ALU.mult),
                waits=[("vec", vv)], inc="vec")
            last = None
            for (alo, ahi) in pieces(0, T):
                last = S.op("tensor", lambda e, alo=alo, ahi=ahi: e.matmul(
                    POUT[:, alo:ahi], ONESROWB[:, :], Rn[:, alo:ahi],
                    start=True, stop=True), waits=[("vec", vv)],
                    inc=("mm" if ahi == T else None))
            nb = S.op("vector", lambda e: e.tensor_copy(NEGBS[:, :], POUT[:, :]),
                      waits=[("mm", last)], inc="vec")
            prev_pa = None
            for j in range(NT):
                last = None
                for (alo, ahi) in pieces(0, T):
                    last = S.op("tensor", lambda e, j=j, alo=alo, ahi=ahi,
                                r0=r0, Q=Q, K=K: e.matmul(
                        PSA[:, alo:ahi], K[r0:r0 + 64, j * 128:(j + 1) * 128],
                        Q[r0:r0 + 64, alo:ahi], start=True, stop=True),
                        waits=([("vec", S.c("vec"))]
                               if alo == 0 else []),
                        inc=("mm" if ahi == T else None))
                vv = S.op("vector", lambda e: e.tensor_tensor(
                    SCC[:, :], PSA[:, :], NEGBS[:, :], ALU.add),
                    waits=[("mm", last)] +
                          ([("act", prev_pa)] if prev_pa else []), inc="vec")
                pa = S.op("scalar", lambda e: e.activation(
                    PTS[:, :], SCC[:, :], AF.Exp), waits=[("vec", vv)],
                    inc="act")
                prev_pa = pa
                for (alo, ahi) in pieces(0, T):
                    S.op("tensor", lambda e, j=j, alo=alo, ahi=ahi, H_=H_:
                         e.matmul(
                        POUT[0:65, alo:ahi],
                        B0[:, j * 1040 + H_ * 65:j * 1040 + (H_ + 1) * 65],
                        PTS[:, alo:ahi], start=(j == 0), stop=(j == NT - 1)),
                        waits=[("act", pa)] if alo == 0 else [],
                        inc=("mm" if (j == NT - 1 and ahi == T) else None))
            mmv = S.c("mm")
            vv = S.op("vector", lambda e: e.reciprocal(Rr, POUT[64:65, 0:2048]),
                      waits=[("mm", mmv)], inc="vec")
            last = None
            for (alo, ahi) in pieces(0, T):
                last = S.op("tensor", lambda e, alo=alo, ahi=ahi: e.matmul(
                    PSA[0:64, alo:ahi], ONESROW[0:1, 0:64], Rr[:, alo:ahi],
                    start=True, stop=True), waits=[("vec", vv)],
                    inc=("mm" if ahi == T else None))
            vv = S.op("vector", lambda e: e.tensor_copy(RECBS[:, :],
                                                        PSA[0:64, :]),
                      waits=[("mm", last)], inc="vec")
            vv = S.op("vector", lambda e: e.tensor_tensor(
                PTS[0:64, :], POUT[0:64, :], RECBS[:, :], ALU.mult), inc="vec")
            dstrip = dram["asp"].rearrange("t r c -> r t c")[64 * hh + 128 * c:
                                                            64 * hh + 128 * c + 64]
            sstrip = PTS[0:64, :].rearrange("p (t c) -> p t c", c=128)
            S.dma("sync", dstrip, sstrip, "d_st", waits=[("vec", vv)])
            S.op("vector", lambda e: e.engine_nop(),
                 waits=S.dtok("d_st"), inc="vec")
        L.done()
        wo_proj(("asp",), dram["F1WoT"], XN1, H1, "f1wo")

    def l1_router():
        S.phase("l1rt")
        L = Locals()
        PL = L.ps("plog1", [128, 512], F32)
        GW = L.sb("GWsb1", [128, 64], BF16)
        LGT = L.sb("LGT1", [128, 8], F32)
        L2T = L.sb("L2T1", [128, 8], F32)
        EQ = L.sb("EQsb1", [128, 8], F32)
        M1H = L.sb("M1H", [128, 8], F32)
        M2H = L.sb("M2H", [128, 8], F32)
        NS = L.sb("NSsb1", [128, 8], F32)
        src = dram["F1gwT"].rearrange("(c p) n -> p c n", p=128)
        ldv = S.dma("gpsimd", GW[:, :].rearrange("p (c n) -> p c n", n=8),
                    src, "d_ld")

        def firsthot(dst, eqb, t):
            pv = S.op("vector", lambda e: e.memset(NS[:, 3:4], 1.0),
                      waits=[("vec", S.c("vec"))], inc="vec")
            for ei in range(NE):
                pv = S.op("vector", lambda e, ei=ei, eqb=eqb: e.tensor_tensor(
                    SCC[:, 8:9], eqb[:, ei:ei + 1], NS[:, 3:4], ALU.mult),
                    waits=[("vec", pv)], inc="vec")
                pv = S.op("vector", lambda e, ei=ei, dst=dst: e.tensor_copy(
                    dst[:, ei:ei + 1], SCC[:, 8:9]),
                    waits=[("vec", pv)], inc="vec")
                pv = S.op("vector", lambda e: e.tensor_tensor(
                    NS[:, 3:4], NS[:, 3:4], SCC[:, 8:9], ALU.subtract),
                    waits=[("vec", pv)], inc="vec")

        for t in range(NT):
            last = None
            for i in range(NDC):
                last = S.op("tensor", lambda e, t=t, i=i: e.matmul(
                    PL[:, 0:8], B2[:, i * 2048 + t * 128:i * 2048 + (t + 1) * 128],
                    GW[:, i * 8:(i + 1) * 8], start=(i == 0),
                    stop=(i == NDC - 1)),
                    waits=([ldv] if (t == 0 and i == 0) else []) +
                          ([("vec", S.c("vec"))] if i == 0 else []),
                    inc=("mm" if i == NDC - 1 else None))
            lv = S.op("vector", lambda e: e.tensor_copy(LGT[:, :], PL[:, 0:8]),
                      waits=[("mm", last)], inc="vec")
            r1 = S.op("vector", lambda e: e.reduce_max(
                NS[:, 0:1], LGT[:, :], axis=AX.X, negate=True),
                waits=[("vec", lv)], inc="vec")
            S.op("vector", lambda e: e.tensor_scalar(
                EQ[:, :], LGT[:, :], NS[:, 0:1], 0.0, ALU.add, ALU.is_ge),
                waits=[("vec", r1)])
            firsthot(M1H, EQ, t)
            pv = S.op("vector", lambda e: e.tensor_scalar(
                SCC[:, 16:24], M1H[:, :], 1e9, None, ALU.mult),
                waits=[("vec", S.c("vec"))], inc="vec")
            pv = S.op("vector", lambda e: e.tensor_tensor(
                L2T[:, :], LGT[:, :], SCC[:, 16:24], ALU.subtract),
                waits=[("vec", pv)], inc="vec")
            r2 = S.op("vector", lambda e: e.reduce_max(
                NS[:, 1:2], L2T[:, :], axis=AX.X, negate=True),
                waits=[("vec", pv)], inc="vec")
            S.op("vector", lambda e: e.tensor_scalar(
                EQ[:, :], L2T[:, :], NS[:, 1:2], 0.0, ALU.add, ALU.is_ge),
                waits=[("vec", r2)])
            firsthot(M2H, EQ, t)
            vv = S.op("vector", lambda e: e.tensor_tensor(
                NS[:, 4:5], NS[:, 1:2], NS[:, 0:1], ALU.subtract),
                waits=[("vec", S.c("vec"))], inc="vec")
            aa = S.op("scalar", lambda e: e.activation(
                NS[:, 5:6], NS[:, 4:5], AF.Exp, scale=-1.0),
                waits=[("vec", vv)], inc="act")
            c1 = S.op("vector", lambda e: e.tensor_scalar(
                NS[:, 6:7], NS[:, 5:6], 1.0, None, ALU.add),
                waits=[("act", aa)], inc="vec")
            c2 = S.op("vector", lambda e: e.reciprocal(NS[:, 7:8], NS[:, 6:7]),
                      waits=[("vec", c1)], inc="vec")
            c3 = S.op("vector", lambda e, t=t: e.tensor_scalar(
                NS[:, 5:6], NS[:, 7:8], -1.0, 1.0, ALU.mult, ALU.add),
                waits=[("vec", c2)], inc="vec")
            c4 = S.op("vector", lambda e, t=t: e.tensor_scalar(
                SCC[:, 16:24], M2H[:, :], NS[:, 5:6], None, ALU.mult),
                waits=[("vec", c3)], inc="vec")
            S.op("vector", lambda e, t=t: e.scalar_tensor_tensor(
                M8[:, t * 8:(t + 1) * 8], M1H[:, :], NS[:, 7:8],
                SCC[:, 16:24], ALU.mult, ALU.add),
                waits=[("vec", c4)], inc="vec")
            if DUMP_SBUF:
                for off, srcb in ((0, LGT), (8, M1H), (16, M2H), (24, NS),
                                  (32, L2T)):
                    S.op("vector", lambda e, t=t, off=off, srcb=srcb:
                         e.tensor_copy(B0[:, t * 64 + off:t * 64 + off + 8],
                                       srcb[:, 0:8]), inc="vec")
        L.done()

    def l1_moe():
        norm_phase(H1, B2, None, "l1rms2", "rms", dram["ln2w"])
        l1_router()
        S.phase("l1moe")
        L = Locals()
        PHa = L.ps("ph1a", [128, 512], F32)
        PHb = L.ps("ph1b", [128, 512], F32)
        PYa = L.ps("py1a", [128, 1024], F32)
        PYb = L.ps("py1b", [128, 1024], F32)
        w1b = [B0[:, 0:8192], B0[:, 8192:16384]]
        w3b = [B1[:, 0:8192], B1[:, 8192:16384]]
        w2b = [B3[:, 0:8192], B3[:, 8192:16384]]
        hb = [WS[:, 0:4096], WS[:, 4096:8192]]
        ld = {}
        mm_after_h = {}
        mm_after_y = {}
        gd_prev = {}

        def issue_loads(key, slot):
            e_, hf = key // 2, key % 2
            w1w = []
            w2w = []
            if key - 2 >= 0:
                w1w = [("mm", mm_after_h[key - 2])]
                w2w = [("mm", mm_after_y[(key - 2, 3)])]
            s1 = dram["F1w1T"][e_][:, hf * 1024:(hf + 1) * 1024]
            s1 = s1.rearrange("(i p) n -> p i n", p=128)
            v1 = S.dma("gpsimd", w1b[slot].rearrange("p (i n) -> p i n", n=1024),
                       s1, "d_w1", waits=w1w)
            s3 = dram["F1w3T"][e_][:, hf * 1024:(hf + 1) * 1024]
            s3 = s3.rearrange("(i p) n -> p i n", p=128)
            v3 = S.dma("gpsimd", w3b[slot].rearrange("p (i n) -> p i n", n=1024),
                       s3, "d_w1", waits=w1w)
            s2 = dram["F1w2T"][e_][hf * 1024:(hf + 1) * 1024, :]
            s2 = s2.rearrange("(f p) n -> p f n", p=128)
            v2 = S.dma("gpsimd", w2b[slot].rearrange("p (f n) -> p f n", n=1024),
                       s2, "d_w2", waits=w2w)
            ld[key] = (v1, v3, v2)

        issue_loads(0, 0)
        issue_loads(1, 1)
        for key in range(NE * 2):
            e_, hf = key // 2, key % 2
            slot = key % 2
            v1, v3, v2 = ld[key]
            for tc in range(4):
                hslot = tc % 2
                hw = []
                pk, ptc = (key, tc - 2) if tc >= 2 else (key - 1, tc + 2)
                if (pk, ptc) in mm_after_y:
                    hw = [("mm", mm_after_y[(pk, ptc)])]
                for fs in range(8):
                    PH = (PHa, PHb)[fs % 2]
                    last = None
                    for i in range(NDC):
                        w_ = []
                        if i == 0:
                            w_ += [("act", S.c("act")), ("vec", S.c("vec"))]
                            if fs == 0:
                                w_ += [v1, v3] + hw
                        last = S.op("tensor", lambda e, fs=fs, tc=tc, i=i,
                                    slot=slot, PH=PH: e.matmul(
                            PH[:, :],
                            w1b[slot][:, i * 1024 + fs * 128:
                                      i * 1024 + (fs + 1) * 128],
                            B2[:, i * 2048 + tc * 512:i * 2048 + (tc + 1) * 512],
                            start=(i == 0), stop=(i == NDC - 1)),
                            waits=w_, inc=("mm" if i == NDC - 1 else None))
                    aa = S.op("scalar", lambda e, PH=PH: e.activation(
                        SCC[:, 0:512], PH[:, :], AF.Silu),
                        waits=[("mm", last)], inc="act")
                    PH3 = (PHa, PHb)[(fs + 1) % 2]
                    last = None
                    for i in range(NDC):
                        last = S.op("tensor", lambda e, fs=fs, tc=tc, i=i,
                                    slot=slot, PH3=PH3: e.matmul(
                            PH3[:, :],
                            w3b[slot][:, i * 1024 + fs * 128:
                                      i * 1024 + (fs + 1) * 128],
                            B2[:, i * 2048 + tc * 512:i * 2048 + (tc + 1) * 512],
                            start=(i == 0), stop=(i == NDC - 1)),
                            waits=([("act", S.c("act"))] if i == 0 else []),
                            inc=("mm" if i == NDC - 1 else None))
                        if i == NDC - 1 and fs == 7:
                            mm_after_h[key] = S.c("mm")
                    S.op("vector", lambda e, fs=fs, hslot=hslot, PH3=PH3,
                         last=last, aa=aa: e.tensor_tensor(
                             hb[hslot][:, fs * 512:(fs + 1) * 512], PH3[:, :],
                             SCC[:, 0:512], ALU.mult),
                         waits=[("mm", last), ("act", aa)], inc="vec")
                hvec = S.c("vec")
                for tsl in range(4):
                    ts = tc * 4 + tsl
                    PY = (PYa, PYb)[ts % 2]
                    last = None
                    for fs in range(8):
                        for seg in range(2):
                            w_ = []
                            if fs == 0 and seg == 0:
                                w_ = [("vec", hvec), v2,
                                      ("vec", S.c("vec"))]
                            last = S.op("tensor", lambda e, fs=fs, seg=seg,
                                        tsl=tsl, slot=slot, hslot=hslot, PY=PY:
                                        e.matmul(
                                PY[:, seg * 512:(seg + 1) * 512],
                                hb[hslot][:, fs * 512 + tsl * 128:
                                          fs * 512 + (tsl + 1) * 128],
                                w2b[slot][:, fs * 1024 + seg * 512:
                                          fs * 1024 + (seg + 1) * 512],
                                start=(fs == 0), stop=(fs == 7)),
                                waits=w_,
                                inc=("mm" if (fs == 7 and seg == 1) else None))
                    mm_after_y[(key, tc)] = last
                    stg = (SCA, SCB)[ts % 2]
                    vv = S.op("vector", lambda e, ts=ts, e_=e_, PY=PY, stg=stg:
                              e.tensor_scalar(
                                  stg[:, :], PY[:, :],
                                  M8[:, ts * 8 + e_:ts * 8 + e_ + 1], None,
                                  ALU.mult),
                              waits=[("mm", last),
                                     gd_prev.get(ts % 2, ("d_gd@g0", 0))],
                              inc="vec")
                    gd_prev[ts % 2] = (f"d_gd@g{ts % 2}", S.op(
                        "gpsimd", lambda e, ts=ts, stg=stg, key=key:
                        e.dma_start(GD[ts * 128:(ts + 1) * 128, :], stg[:, :],
                                    accum_op=(ALU.bypass if key == 0
                                              else ALU.add)),
                        waits=[("vec", vv)], inc=f"d_gd@g{ts % 2}", inc_amt=16))
            if key + 2 < NE * 2:
                issue_loads(key + 2, (key + 2) % 2)
        L.done()

    def l1_combine():
        S.phase("l1cmb")
        st_prev = ("d_st@sync", 0)
        for t in range(NT):
            h1s = (XTA, XTB)[t % 2]
            ldh = S.dma("sync", h1s[:, :], H1[t * 128:(t + 1) * 128, :], "d_ld",
                        waits=([("vec", S.c("vec"))] if t >= 2 else ()))
            gs = (SCA, SCB)[t % 2]
            ldg = S.dma("scalar", gs[:, :], GD[t * 128:(t + 1) * 128, :], "d_ld",
                        waits=S.dtok("d_gd") if t == 0 else ())
            lds = S.dma("sync", ADD[:, :], S1[t * 128:(t + 1) * 128, :], "d_ld",
                        waits=([("vec", S.c("vec"))] if t >= 1 else ()))
            vv = S.op("vector", lambda e, h1s=h1s, gs=gs: e.tensor_tensor(
                SCC[:, 0:1024], h1s[:, :], gs[:, :], ALU.add),
                waits=S.dtok("d_ld"), inc="vec")
            aa = S.op("scalar", lambda e: e.activation(
                WOB[:, :], SCC[:, 0:1024], AF.Silu), waits=[("vec", vv)],
                inc="act")
            vv = S.op("vector", lambda e: e.tensor_tensor(
                SCC[:, 0:1024], ADD[:, :], WOB[:, :], ALU.mult),
                waits=[("act", aa)], inc="vec")
            ldx = S.dma("scalar", gs[:, :], XRES[t * 128:(t + 1) * 128, :],
                        "d_ld", waits=[("vec", vv)])
            vv = S.op("vector", lambda e, gs=gs: e.tensor_tensor(
                SCC[:, 1024:2048], gs[:, :], SCC[:, 0:1024], ALU.add),
                waits=[ldx, st_prev], inc="vec")
            st_prev = S.dma("sync", dram["y"][t * 128:(t + 1) * 128, :],
                            SCC[:, 1024:2048], "d_st", waits=[("vec", vv)])
        S.op("sync", lambda e: e.nop(), waits=S.dtok("d_st"))

    load_consts()
    if 0 in layers:
        norm_phase(dram["x"], B2, None, "ln0", "ln")
        l0_attn()
        l0_router()
        l0_ffn()
        l0_combine()
    if 1 in layers:
        norm_phase(XRES, B2, (None if "xn1store" in SKIP else XN1), "ln1", "ln")
        if "ret" not in SKIP:
            l1_retention()
        l1_softmax_attn()
        if "moe" not in SKIP:
            l1_moe()
        if "cmb" not in SKIP:
            l1_combine()

    if stop_after is not None:
        idx = max(i for i, lab in S.labels if lab == stop_after)
        S.phases = S.phases[:idx + 1]
        if DUMP_SBUF:
            S.phases.append([])
            for nm, buf, w in (("dbgB0", B0, 16640), ("dbgB1", B1, 16384),
                               ("dbgB2", B2, 16384), ("dbgB3", B3, 16384),
                               ("dbgWS", WS, 8192)):
                d = nc.dram_tensor(nm, [128, w], BF16, kind="ExternalOutput").ap()
                S.dma("sync", d, buf[:, 0:w], "d_dbg")
            d = nc.dram_tensor("dbgM8", [128, 128], F32,
                               kind="ExternalOutput").ap()
            S.dma("sync", d, M8[:, :], "d_dbg")
            S.op("sync", lambda e: e.nop(), waits=S.dtok("d_dbg"))
    S.emit()
    return nc, dram


def _prep_shared(inputs, layers):
    f = np.float32
    sh = {}

    def tr(a):
        return np.ascontiguousarray(np.asarray(a, dtype=f).T)

    if 0 in layers:
        sh["L0WqT"] = tr(inputs["L0_attn_Wq"])
        sh["L0WkT"] = tr(inputs["L0_attn_Wk"])
        sh["L0WvT"] = tr(inputs["L0_attn_Wv"])
        sh["L0WoT"] = tr(inputs["L0_attn_Wo"])
        sh["L0WfT"] = tr(inputs["L0_attn_Wf"])
        sh["L0gw"] = np.ascontiguousarray(np.asarray(inputs["L0_ffn_gate_w"],
                                                     dtype=f))
        sh["L0w1"] = np.ascontiguousarray(np.asarray(inputs["L0_ffn_w1"],
                                                     dtype=f))
        sh["L0b1"] = np.ascontiguousarray(
            np.asarray(inputs["L0_ffn_b1"], dtype=f).reshape(NE, 32, 128))
        sh["L0w2"] = np.ascontiguousarray(np.asarray(inputs["L0_ffn_w2"],
                                                     dtype=f))
        sh["L0b2"] = np.ascontiguousarray(np.asarray(inputs["L0_ffn_b2"],
                                                     dtype=f))
    if 1 in layers:
        sh["L1WqT"] = tr(inputs["L1_attn_wq"])
        sh["L1WkT"] = tr(inputs["L1_attn_wk"])
        sh["L1WvT"] = tr(inputs["L1_attn_wv"])
        sh["L1WoT"] = tr(inputs["L1_attn_wo"])
        sh["gammas"] = np.ascontiguousarray(
            np.asarray(inputs["L1_attn_gammas"], dtype=f).reshape(1, NH))
        sh["ln1w"] = np.ascontiguousarray(
            np.asarray(inputs["L1_ffn_ln1_w"], dtype=f).reshape(1, D))
        sh["ln2w"] = np.ascontiguousarray(
            np.asarray(inputs["L1_ffn_ln2_w"], dtype=f).reshape(1, D))
        sh["F1WqT"] = tr(inputs["L1_ffn_wq"])
        sh["F1WkT"] = tr(inputs["L1_ffn_wk"])
        sh["F1WvT"] = tr(inputs["L1_ffn_wv"])
        sh["F1WoT"] = tr(inputs["L1_ffn_wo"])
        sh["F1gwT"] = tr(inputs["L1_ffn_gate_w"])
        sh["F1w1T"] = np.ascontiguousarray(
            np.asarray(inputs["L1_ffn_w1"], dtype=f).transpose(0, 2, 1))
        sh["F1w3T"] = np.ascontiguousarray(
            np.asarray(inputs["L1_ffn_w3"], dtype=f).transpose(0, 2, 1))
        sh["F1w2T"] = np.ascontiguousarray(
            np.asarray(inputs["L1_ffn_w2"], dtype=f).transpose(0, 2, 1))
    sh["IDBF"] = np.eye(128, dtype=BF)
    sh["IDF32"] = np.eye(128, dtype=f)
    sh["ONESROW"] = np.ones((1, 128), dtype=f)
    sh["ONESROWB"] = np.ones((1, 128), dtype=BF)
    sh["ONESCOL"] = np.ones((128, 1), dtype=BF)
    sh["PIDX"] = np.arange(128, dtype=f).reshape(128, 1)
    sh["UROW"] = np.broadcast_to(np.arange(T, dtype=f), (128, T)).astype(BF)
    sh["UROW"] = np.ascontiguousarray(sh["UROW"])
    ct = (np.arange(128)[None, :] >= np.arange(128)[:, None])
    sh["CTRI"] = np.ascontiguousarray(ct.astype(BF))
    return sh


def kernel(**inputs):
    x = np.ascontiguousarray(np.asarray(inputs["L0_attn_hidden_states"],
                                        dtype=np.float32))
    ncores = 8
    if not UNFUSED:
        sh = _prep_shared(inputs, (0, 1))
        nc, _ = build_core_program((0, 1))
        in_maps = [dict(sh, x=np.ascontiguousarray(x[b]))
                   for b in range(ncores)]
        res = run_bass_kernel_spmd(nc, in_maps, core_ids=list(range(ncores)))
        return np.stack([np.asarray(res.results[b]["y"], dtype=np.float32)
                         for b in range(ncores)], axis=0)
    sh0 = _prep_shared(inputs, (0,))
    nc0, _ = build_core_program((0,))
    maps0 = [dict(sh0, x=np.ascontiguousarray(x[b])) for b in range(ncores)]
    r0 = run_bass_kernel_spmd(nc0, maps0, core_ids=list(range(ncores)))
    x1 = [np.asarray(r0.results[b]["xout"], dtype=np.float32)
          for b in range(ncores)]
    sh1 = _prep_shared(inputs, (1,))
    nc1, _ = build_core_program((1,))
    maps1 = [dict(sh1, xin1=np.ascontiguousarray(x1[b]))
             for b in range(ncores)]
    r1 = run_bass_kernel_spmd(nc1, maps1, core_ids=list(range(ncores)))
    return np.stack([np.asarray(r1.results[b]["y"], dtype=np.float32)
                     for b in range(ncores)], axis=0)
